# version 14
# baseline (speedup 1.0000x reference)
import contextlib
import numpy as np
import ml_dtypes
import concourse.bass as bass
import concourse.mybir as mybir
from concourse.bass_utils import run_bass_kernel_spmd

F32 = mybir.dt.float32
BF16 = mybir.dt.bfloat16
FP16 = mybir.dt.float16
F8E4 = mybir.dt.float8e4
I32 = mybir.dt.int32
U8 = mybir.dt.uint8
AF = mybir.ActivationFunctionType
ALU = mybir.AluOpType
AX = mybir.AxisListType

B, T, D = 4, 4096, 1024
HALF = 2048
HALO = 256
TOK = HALF + HALO
NCHUNK = 8
EPS = 1e-5

ENGS = ("sy", "gp", "pe", "sc", "ve")
SEMS = ("dx", "dw", "da", "db", "pe", "sc", "ve", "gp")


def _tiles(lo, hi, w=512):
    out = []
    o = lo
    while o < hi:
        out.append((o, min(w, hi - o)))
        o += w
    return out


class _DummyInstr:
    def then_inc(self, *a, **k):
        return self


class _DummyAny:
    def __call__(self, *a, **k):
        return self
    def __getattr__(self, name):
        return self
    def __enter__(self):
        return self
    def __exit__(self, *a):
        return False


class Ctx:

    def __init__(self, nc, sems):
        self.nc = nc
        self.sems = sems
        self.cnt = {s: 0 for s in SEMS}
        self.marks = {}
        self.live = None
        self.eng_handles = {}

    def begin_pass(self, live, cnt0):
        self.live = live
        self.cnt = dict(cnt0)

    def eng(self, name):
        if self.live == name:
            return self.eng_handles[name]
        return _DummyAny()

    def inc(self, name, instr, sem, n=1, tag=None):
        self.cnt[sem] += n
        if self.live == name:
            assert not isinstance(instr, _DummyAny), \
                f"inc({name},{sem},{tag}): dummy instruction in live pass ({tag})"
            instr.then_inc(self.sems[sem], n)
            self.emitted[sem] = self.emitted.get(sem, 0) + n
        if tag is not None:
            self.marks[tag] = (sem, self.cnt[sem])
        return self.cnt[sem]

    def mark(self, tag, sem):
        self.marks[tag] = (sem, self.cnt[sem])

    def wait(self, name, tag):
        if tag not in self.marks:
            assert self.live is None, f"unknown mark {tag} in emit pass"
            return
        sem, v = self.marks[tag]
        if sem in ("dx", "dw"):
            v = self.cnt[sem]
        if self.live == name:
            self.eng_handles[name].wait_ge(self.sems[sem], v)

    def chain(self, name, instr):
        sem = {"sy": "dx", "gp": "gp", "pe": "pe", "sc": "sc", "ve": "ve"}[name]
        v = self.inc(name, instr, sem)
        if self.live == name:
            self.eng_handles[name].wait_ge(self.sems[sem], v)

    def wait_all(self, name):
        if self.live == name:
            e = self.eng_handles[name]
            for s in SEMS:
                if self.cnt[s] > 0:
                    e.wait_ge(self.sems[s], self.cnt[s])



def ln_tile(ctx, H, ph, o, w, out_bf=None, out_f32=None, out_off=0):
    sy, gp, pe, sc, ve = (ctx.eng(e) for e in ENGS)
    x = H["x"]
    xb, sq = H["xb"], H["sq"]
    rows = H["rows"]
    psm, psv, psrB, psmrB = H["ps"][0], H["ps"][1], H["ps"][2], H["ps"][3]
    prev = H.get("_ln_prev")
    if prev is not None:
        ctx.wait("ve", f"{prev}.stats")
        ctx.wait("sc", f"{prev}.stats")
        ctx.wait("pe", f"{prev}.rows")
        ctx.wait("pe", f"{prev}.m2")
        ctx.wait("sc", f"{prev}.var")
        ctx.wait("ve", f"{prev}.bcast")
        ctx.wait("pe", f"{prev}.done")
    for eng_t, t in H.get("_ln_cons", ()):
        ctx.wait(eng_t, t)
    H["_ln_cons"] = ()
    H["_ln_prev"] = ph

    for c in range(NCHUNK):
        i = ve.tensor_copy(xb[:, c, 0:w], x[:, c, o:o + w])
    ctx.inc("ve", i, "ve", tag=f"{ph}.xbdone")
    for c in range(NCHUNK):
        i = sc.activation(sq[:, c, 0:w], x[:, c, o:o + w], AF.Square)
    ctx.inc("sc", i, "sc", tag=f"{ph}.sq")
    ctx.wait("pe", f"{ph}.xbdone")
    ctx.wait("pe", f"{ph}.sq")
    for c in range(NCHUNK):
        pe.matmul(psm[0:1, 0:w], H["statcol"][:], xb[:, c, 0:w],
                  start=(c == 0), stop=(c == NCHUNK - 1))
    for c in range(NCHUNK):
        i = pe.matmul(psv[0:1, 0:w], H["statcol"][:], sq[:, c, 0:w],
                      start=(c == 0), stop=(c == NCHUNK - 1))
    ctx.inc("pe", i, "pe", tag=f"{ph}.stats")
    ctx.wait("sc", f"{ph}.stats")
    i = sc.activation(rows[0:1, 0:w], psm[0:1, 0:w], AF.Square)
    ctx.inc("sc", i, "sc", tag=f"{ph}.m2")
    ctx.wait("ve", f"{ph}.m2")
    i = ve.scalar_tensor_tensor(rows[0:1, 512:512 + w], psv[0:1, 0:w], 1.0,
                                rows[0:1, 0:w], ALU.mult, ALU.subtract)
    ctx.inc("ve", i, "ve", tag=f"{ph}.var")
    ctx.wait("sc", f"{ph}.var")
    i = sc.activation(rows[0:1, 0:w], rows[0:1, 512:512 + w],
                      AF.Ln, bias=H["epsc"][:])
    ctx.chain("sc", i)
    i = sc.activation(H["rows"][0:1, 1024:1024 + w], rows[0:1, 0:w],
                      AF.Exp, scale=-0.5)
    ctx.inc("sc", i, "sc", tag=f"{ph}.sd")
    ctx.wait("ve", f"{ph}.sd")
    i = ve.tensor_tensor(H["rows"][0:1, 512:512 + w], psm[0:1, 0:w],
                         H["rows"][0:1, 1024:1024 + w], ALU.mult)
    ctx.inc("ve", i, "ve", tag=f"{ph}.rows")
    ctx.wait("pe", f"{ph}.rows")
    ctx.wait("pe", f"{ph}.sd")
    pe.matmul(psrB[:, 0:w], H["onesc1f"][:], H["rows"][0:1, 1024:1024 + w],
              start=True, stop=True)
    i = pe.matmul(psmrB[:, 0:w], H["onesc1f"][:], H["rows"][0:1, 512:512 + w],
                  start=True, stop=True)
    ctx.inc("pe", i, "pe", tag=f"{ph}.bcast")
    ctx.wait("ve", f"{ph}.bcast")
    st = H["stF"]
    for c in range(NCHUNK):
        i = ve.tensor_tensor(st[0:, 0:w], x[:, c, o:o + w], psrB[:, 0:w],
                             ALU.mult)
        ctx.chain("ve", i)
        if out_bf is not None:
            i = ve.tensor_tensor(out_bf[:, c, out_off:out_off + w], st[0:, 0:w],
                                 psmrB[:, 0:w], ALU.subtract)
        if out_f32 is not None:
            i = ve.tensor_tensor(out_f32[c], st[0:, 0:w],
                                 psmrB[:, 0:w], ALU.subtract)
        ctx.chain("ve", i)
    ctx.mark(f"{ph}.done", "ve")


def phase_init(ctx, H):
    sy, gp, pe, sc, ve = (ctx.eng(e) for e in ENGS)
    i = sy.dma_start(out=H["x"][:],
                     in_=H["in"]["xT"].rearrange("(c p) n -> p c n", p=128))
    ctx.inc("sy", i, "dx", 16, tag="init.x")
    i = sy.dma_start(out=H["cg"][:], in_=H["in"]["cg"])
    ctx.inc("sy", i, "dx", 16)
    i = sy.dma_start(out=H["noncet"][:], in_=H["in"]["nonce"])
    ctx.inc("sy", i, "dx", 16)
    i = sy.dma_start(out=H["hxt"][:], in_=H["in"]["halfidx"])
    ctx.inc("sy", i, "dx", 16)
    ve.memset(H["statcol"][:], 1.0 / 1024.0)
    ve.memset(H["onesc1f"][:], 1.0)
    ve.memset(H["onesrB"][:], 1.0)
    ve.memset(H["onesc1B"][:], 1.0)
    ve.memset(H["onesrF8"][:], 1.0)
    ve.memset(H["neg8"][:], -8.0)
    i = ve.memset(H["epsc"][:], EPS)
    ctx.inc("ve", i, "ve", tag="init.consts")


def phase_l0ffn(ctx, H, which, lo, hi):
    sy, gp, pe, sc, ve = (ctx.eng(e) for e in ENGS)
    F = H["f0"]
    for nm, dst in (("Win", F["WinB"]), ("scat", F["scatB"])):
        i = gp.dma_start(out=dst[:], in_=H["in"][nm].rearrange(
            "(c p) n -> p c n", p=128))
        ctx.inc("gp", i, "dw", 16)
    for nm, dst in (("E", F["EB"]), ("Wout", F["WoutB"])):
        i = gp.dma_start(out=dst[:], in_=H["in"][nm].rearrange(
            "p (c n) -> p c n", n=128))
        ctx.inc("gp", i, "dw", 16)
    i = gp.dma_start(out=F["W1c"][:], in_=H["in"]["W1b"])
    ctx.inc("gp", i, "dw", 16)
    i = gp.dma_start(out=F["b1c"][:], in_=H["in"]["b1b"])
    ctx.inc("gp", i, "dw", 16)
    for nm, off, n in (("binr", 0, 64), ("b2r", 64, 64), ("boutr", 128, 1024)):
        i = gp.dma_start(out=F["rbf"][0:1, off:off + n], in_=H["in"][nm])
        ctx.inc("gp", i, "dw", 16)
    ctx.mark(f"{which}.w", "dw")
    for e in ("pe", "ve", "sc"):
        ctx.wait(e, f"{which}.w")
        ctx.wait(e, "init.consts")
        ctx.wait(e, "init.x")
    ps = H["ps"]
    for ti, (o, w) in enumerate(_tiles(lo, hi)):
        ph = f"{which}.t{ti}"
        pv = f"{which}.t{ti-1}"
        ln_tile(ctx, H, ph, o, w, out_bf=F["xnB"])
        H["_ln_cons"] = (("ve", f"{ph}.s"),)
        ctx.wait("pe", f"{ph}.done")
        if ti > 0:
            ctx.wait("pe", f"{pv}.gevac")
            ctx.wait("pe", f"{pv}.o{NCHUNK-1}")
        for c in range(NCHUNK):
            pe.matmul(ps[4][0:64, 0:w], F["WinB"][:, c], F["xnB"][:, c, 0:w],
                      start=(c == 0), stop=False)
        i = pe.matmul(ps[4][0:64, 0:w], F["rbf"][0:1, 0:64],
                      H["onesrB"][0:1, 0:w], start=False, stop=True)
        ctx.inc("pe", i, "pe", tag=f"{ph}.s")
        ctx.wait("ve", f"{ph}.s")
        if ti > 0:
            ctx.wait("ve", f"{pv}.hmm{NCHUNK-1}")
        i = ve.tensor_copy(F["sSB"][0:64, 0:w], ps[4][0:64, 0:w])
        ctx.inc("ve", i, "ve", tag=f"{ph}.sevac")
        for c8 in range(NCHUNK):
            bank = ps[5 - (c8 % 2)]
            ctx.wait("pe", f"{ph}.sevac")
            if c8 >= 2:
                ctx.wait("pe", f"{ph}.h{c8-2}")
            elif ti > 0:
                ctx.wait("pe", f"{pv}.o{6 + c8}")
            if c8 == 0 and ti > 0:
                ctx.wait("sc", f"{pv}.g")
            i = pe.matmul(bank[:, 0:w], F["EB"][0:64, c8], F["sSB"][0:64, 0:w],
                          start=True, stop=True)
            ctx.inc("pe", i, "pe", tag=f"{ph}.hmm{c8}")
            ctx.wait("sc", f"{ph}.hmm{c8}")
            i = sc.activation(F["hSB"][:, c8, 0:w], bank[:, 0:w], AF.Relu,
                              scale=F["W1c"][:, c8:c8 + 1],
                              bias=F["b1c"][:, c8:c8 + 1])
            ctx.inc("sc", i, "sc", tag=f"{ph}.h{c8}")
        for c8 in range(NCHUNK):
            ctx.wait("pe", f"{ph}.h{NCHUNK-1 if c8 == 0 else c8}")
            pe.matmul(ps[4][0:64, 0:w], F["scatB"][:, c8], F["hSB"][:, c8, 0:w],
                      start=(c8 == 0), stop=False)
        i = pe.matmul(ps[4][0:64, 0:w], F["rbf"][0:1, 64:128],
                      H["onesrB"][0:1, 0:w], start=False, stop=True)
        ctx.inc("pe", i, "pe", tag=f"{ph}.g")
        ctx.wait("ve", f"{ph}.g")
        if ti > 0:
            ctx.wait("ve", f"{pv}.omm{NCHUNK-1}")
        i = ve.tensor_copy(F["gSB"][0:64, 0:w], ps[4][0:64, 0:w])
        ctx.inc("ve", i, "ve", tag=f"{ph}.gevac")
        for oc in range(NCHUNK):
            bank = ps[5 - (oc % 2)]
            ctx.wait("pe", f"{ph}.gevac")
            if oc >= 2:
                ctx.wait("pe", f"{ph}.o{oc-2}")
            else:
                ctx.wait("pe", f"{ph}.h{6 + oc}")
            pe.matmul(bank[:, 0:w], F["WoutB"][0:64, oc], F["gSB"][0:64, 0:w],
                      start=True, stop=False)
            i = pe.matmul(bank[:, 0:w],
                          F["rbf"][0:1, 128 + 128 * oc:128 + 128 * (oc + 1)],
                          H["onesrB"][0:1, 0:w], start=False, stop=True)
            ctx.inc("pe", i, "pe", tag=f"{ph}.omm{oc}")
            ctx.wait("ve", f"{ph}.omm{oc}")
            i = ve.scalar_tensor_tensor(H["x"][:, oc, o:o + w], bank[:, 0:w],
                                        0.5, H["x"][:, oc, o:o + w],
                                        ALU.mult, ALU.add)
            ctx.inc("ve", i, "ve", tag=f"{ph}.o{oc}")


def phase_out(ctx, H):
    sy, gp, pe, sc, ve = (ctx.eng(e) for e in ENGS)
    ctx.wait_all("sy")
    i = sy.dma_start(out=H["out"]["y"].rearrange("(c p) n -> p c n", p=128),
                     in_=H["x"][:, :, HALO:TOK])
    ctx.inc("sy", i, "dx", 16, tag="out.y")
    ctx.wait("sy", "out.y")



def build_program(phases):
    import os
    nc = bass.Bass("TRN2", target_bir_lowering=False,
                   detect_race_conditions=bool(os.environ.get("KRACE")))
    IN = {}
    def inp(name, shape, dtype=F32):
        IN[name] = nc.dram_tensor(name, shape, dtype, kind="ExternalInput").ap()
    inp("xT", [1024, TOK])
    for nm in ("qT", "kT", "vW", "oT"):
        inp(nm, [1024, 1024])
    for nm, shp in (("qb", [1, 1024]), ("kb", [1, 1024]), ("vb", [1, 1024]),
                    ("ob", [1, 1024]), ("Win", [1024, 64]), ("binr", [1, 64]),
                    ("E", [64, 1024]), ("W1b", [128, 8]), ("b1b", [128, 8]),
                    ("scat", [1024, 64]), ("b2r", [1, 64]), ("Wout", [64, 1024]),
                    ("boutr", [1, 1024]), ("w1T", [1024, 4096]),
                    ("b1r", [1, 4096]), ("w2T", [4096, 1024]),
                    ("b2r1", [1, 1024]), ("W0z", [1024, 3072]),
                    ("W1z", [1024, 3072]), ("cg", [128, 1])):
        inp(nm, shp)
    for nm in ("maskA", "maskB", "maskC"):
        inp(nm, [128, 384], BF16)
    inp("dmask", [1024, HALF], BF16)
    inp("ident", [128, 128], BF16)
    inp("nonce", [1, 2], I32)
    IN["halfidx"] = nc.dram_tensor("halfidx", [1, 1], I32, kind="ExternalInput").ap()
    OUT = {"y": nc.dram_tensor("y", [1024, HALF], F32, kind="ExternalOutput").ap()}
    import os
    if os.environ.get("KDBG_ATT"):
        for nm, shp, dt_ in (("dbg_q", [128, 4, 512], BF16), ("dbg_k", [128, 4, 768], BF16),
                             ("dbg_vT", [128, 6, 512], BF16), ("dbg_at", [128, 384], BF16),
                             ("dbg_ex", [128, 384], BF16), ("dbg_oin", [128, 4, 128], BF16),
                             ("dbg_xn", [128, 8, 512], BF16)):
            OUT[nm] = nc.dram_tensor(nm, shp, dt_, kind="ExternalOutput").ap()
    xpage = nc.dram_tensor("xpage", [1024, HALF], F32)
    shm_c = nc.dram_tensor("shm_c", [2, 2, 1024], F32, addr_space="Shared")
    shm_f = nc.dram_tensor("shm_f", [2, 2, 1], I32, addr_space="Shared")

    with contextlib.ExitStack() as es:
        E = es.enter_context
        H = {"in": IN, "out": OUT, "xpage": xpage, "shm_c": shm_c, "shm_f": shm_f}
        H["x"] = E(nc.sbuf_tensor("xbuf", [128, NCHUNK, TOK], F32))
        H["xb"] = E(nc.sbuf_tensor("xbB", [128, NCHUNK, 512], BF16))
        H["sq"] = E(nc.sbuf_tensor("sqB", [128, NCHUNK, 512], BF16))
        H["stF"] = E(nc.sbuf_tensor("stF", [128, 512], F32))
        H["rows"] = E(nc.sbuf_tensor("rows", [1, 1536], F32))
        H["statcol"] = E(nc.sbuf_tensor("statcol", [128, 1], BF16))
        H["onesc1f"] = E(nc.sbuf_tensor("onesc1f", [1, 128], F32))
        H["onesrB"] = E(nc.sbuf_tensor("onesrB", [1, 512], BF16))
        H["onesc1B"] = E(nc.sbuf_tensor("onesc1B", [1, 128], BF16))
        H["onesrF8"] = E(nc.sbuf_tensor("onesrF8", [1, 512], F8E4))
        H["epsc"] = E(nc.sbuf_tensor("epsc", [1, 1], F32))
        H["neg8"] = E(nc.sbuf_tensor("neg8", [128, 1], F32))
        H["cg"] = E(nc.sbuf_tensor("sb_cg", [128, 1], F32))
        H["noncet"] = E(nc.sbuf_tensor("noncet", [1, 2], I32))
        H["hxt"] = E(nc.sbuf_tensor("hxt", [1, 1], I32))
        H["ps"] = [E(nc.psum_tensor(f"psb{i}", [128, 512], F32)) for i in range(8)]

        sems = {s: E(nc.semaphore(f"sem_{s}")) for s in SEMS}
        sems["pl"] = E(nc.semaphore("sem_pl"))
        block = E(nc.Block())

        ctx = Ctx(nc, sems)

        def run_phase(fn, barrier=True):
            cnt0 = dict(ctx.cnt)
            state0 = {k: H[k] for k in H if k.startswith("_")}
            def reset_state():
                for k in [k for k in H if k.startswith("_")]:
                    del H[k]
                H.update(state0)
            ctx.begin_pass(None, cnt0)
            fn()
            plan_final = dict(ctx.cnt)
            for name, reg in (("sy", block.sync), ("gp", block.gpsimd),
                              ("pe", block.tensor), ("sc", block.scalar),
                              ("ve", block.vector)):
                def body(e, name=name, cnt0=cnt0, fn=fn):
                    ctx.eng_handles[name] = e
                    ctx.begin_pass(name, cnt0)
                    reset_state()
                    if barrier:
                        ctx.wait_all(name)
                    ctx.emitted = {}
                    fn()
                    own = {"sy": ("dx",), "gp": ("dw", "da", "db"),
                           "pe": ("pe",), "sc": ("sc",), "ve": ("ve",)}[name]
                    for s_ in own:
                        want = plan_final[s_] - cnt0[s_]
                        got = ctx.emitted.get(s_, 0)
                        assert got == want, (
                            f"pass {name} sem {s_}: emitted {got} != planned {want}")
                    assert ctx.cnt == plan_final, (
                        f"pass {name}: counters diverged "
                        f"{ {k: (ctx.cnt[k], plan_final[k]) for k in ctx.cnt if ctx.cnt[k] != plan_final[k]} }")
                reg(body)
            ctx.live = None

        def scoped(names):
            s = contextlib.ExitStack()
            return s

        for p in phases:
            if p == "init":
                run_phase(lambda: phase_init(ctx, H), barrier=False)
            elif p in ("f0a", "f0b"):
                with contextlib.ExitStack() as sp:
                    Ep = sp.enter_context
                    H["f0"] = {
                        "WinB": Ep(nc.sbuf_tensor(f"{p}_WinB", [128, 8, 64], BF16)),
                        "EB": Ep(nc.sbuf_tensor(f"{p}_EB", [64, 8, 128], BF16)),
                        "scatB": Ep(nc.sbuf_tensor(f"{p}_scatB", [128, 8, 64], BF16)),
                        "WoutB": Ep(nc.sbuf_tensor(f"{p}_WoutB", [64, 8, 128], BF16)),
                        "W1c": Ep(nc.sbuf_tensor(f"{p}_W1c", [128, 8], F32)),
                        "b1c": Ep(nc.sbuf_tensor(f"{p}_b1c", [128, 8], F32)),
                        "rbf": Ep(nc.sbuf_tensor(f"{p}_rbf", [1, 1152], BF16)),
                        "sSB": Ep(nc.sbuf_tensor(f"{p}_sSB", [128, 512], BF16)),
                        "gSB": Ep(nc.sbuf_tensor(f"{p}_gSB", [64, 512], BF16)),
                        "hSB": Ep(nc.sbuf_tensor(f"{p}_hSB", [128, 8, 512], BF16)),
                        "xnB": Ep(nc.sbuf_tensor(f"{p}_xnB", [128, 8, 512], BF16)),
                    }
                    lo = 0 if p == "f0a" else HALO
                    run_phase(lambda p=p, lo=lo: phase_l0ffn(ctx, H, p, lo, TOK))
                    del H["f0"]
            elif p == "att":
                with contextlib.ExitStack() as sp:
                    Ep = sp.enter_context
                    H["att"] = {
                        "qTh": Ep(nc.sbuf_tensor("at_qTh", [128, 8, 512], BF16)),
                        "kTh": Ep(nc.sbuf_tensor("at_kTh", [128, 8, 512], BF16)),
                        "vWh": Ep(nc.sbuf_tensor("at_vWh", [128, 8, 512], BF16)),
                        "oTh": Ep(nc.sbuf_tensor("at_oTh", [128, 4, 1024], BF16)),
                        "qbuf": Ep(nc.sbuf_tensor("at_qbuf", [128, 4, 512], BF16)),
                        "kbuf": Ep(nc.sbuf_tensor("at_kbuf", [128, 4, 768], BF16)),
                        "vTbuf": Ep(nc.sbuf_tensor("at_vT", [128, 6, 8, 66], BF16)),
                        "o_in": Ep(nc.sbuf_tensor("at_oin", [128, 2, 4, 128], BF16)),
                        "rowsA": Ep(nc.sbuf_tensor("at_rows", [1, 2560], BF16)),
                        "expA": Ep(nc.sbuf_tensor("at_expA", [128, 8, 384], BF16)),
                        "oraw": Ep(nc.sbuf_tensor("at_oraw", [65, 4, 128], F32)),
                        "rrow": Ep(nc.sbuf_tensor("at_rrow", [1, 2, 128], BF16)),
                        "lnw": Ep(nc.sbuf_tensor("at_lnw", [1, 2, 128], F32)),
                        "maskA": Ep(nc.sbuf_tensor("at_maskA", [128, 384], BF16)),
                        "maskB": Ep(nc.sbuf_tensor("at_maskB", [128, 384], BF16)),
                        "maskC": Ep(nc.sbuf_tensor("at_maskC", [128, 384], BF16)),
                        "xnf": Ep(nc.sbuf_tensor("at_xnf", [128, 8, TOK], BF16)),
                    }
                    run_phase(lambda: phase_attn(ctx, H))
                    del H["att"]
            elif p in ("f1a", "f1b"):
                with contextlib.ExitStack() as sp:
                    Ep = sp.enter_context
                    H["f1"] = {
                        "w1q": Ep(nc.sbuf_tensor(f"{p}_w1q", [128, 8, 4096], F8E4)),
                        "w2q": Ep(nc.sbuf_tensor(f"{p}_w2q", [128, 32, 1024], F8E4)),
                        "xq": Ep(nc.sbuf_tensor(f"{p}_xq", [128, 8, 512], F8E4)),
                        "hq": Ep(nc.sbuf_tensor(f"{p}_hq", [128, 32, 512], F8E4)),
                        "y16": Ep(nc.sbuf_tensor(f"{p}_y16", [128, 512], FP16)),
                        "h16": Ep(nc.sbuf_tensor(f"{p}_h16", [128, 512], FP16)),
                        "dm": Ep(nc.sbuf_tensor(f"{p}_dm", [128, 8, 512], BF16)),
                        "rf8": Ep(nc.sbuf_tensor(f"{p}_rf8", [1, 5120], F8E4)),
                    }
                    run_phase(lambda p=p: phase_l1ffn(ctx, H, p))
                    del H["f1"]
            elif p == "scan":
                with contextlib.ExitStack() as sp:
                    Ep = sp.enter_context
                    H["sn"] = {
                        "Wz": [Ep(nc.sbuf_tensor(f"sn_Wz{j}", [128, 8, 384], BF16))
                               for j in range(2)],
                        "gA": Ep(nc.sbuf_tensor("sn_gA", [128, 8, 2048], BF16)),
                        "f": Ep(nc.sbuf_tensor("sn_f", [128, 8, 512], F32)),
                        "msk": Ep(nc.sbuf_tensor("sn_msk", [128, 512], U8)),
                        "carH": Ep(nc.sbuf_tensor("sn_carH", [128, 8], F32)),
                        "carA": Ep(nc.sbuf_tensor("sn_carA", [128, 8], F32)),
                        "Hin": Ep(nc.sbuf_tensor("sn_Hin", [128, 8], F32)),
                        "flag": Ep(nc.sbuf_tensor("sn_flag", [1, 1], I32)),
                        "hnF": Ep(nc.sbuf_tensor("sn_hnF", [128, 8, 2048], BF16)),
                        "xst": Ep(nc.sbuf_tensor("sn_xst", [128, 2048], F32)),
                        "zeros512": Ep(nc.sbuf_tensor("zeros512", [128, 512], F32)),
                    }
                    run_phase(lambda: phase_scan(ctx, H))
                    del H["sn"]
            elif p == "out":
                run_phase(lambda: phase_out(ctx, H))
            else:
                raise ValueError(p)
    return nc



_BUILD_CACHE = {}

def phase_scan(ctx, H):
    sy, gp, pe, sc, ve = (ctx.eng(e) for e in ENGS)
    ps = H["ps"]
    S = H["sn"]
    x = H["x"]
    xpage = H["xpage"]

    for c in range(NCHUNK):
        i = sy.dma_start(out=xpage[128 * c:128 * (c + 1), :], in_=x[:, c, HALO:TOK])
        ctx.inc("sy", i, "dx", 16)
    ctx.mark("scan.pgout", "dx")
    for e in ("pe", "ve", "sc"):
        ctx.wait(e, "init.consts")
    i = ve.memset(S["zeros512"][:], 0.0)
    ctx.inc("ve", i, "ve", tag="scan.z0")
    H["_ln_cons"] = (("ve", "scan.pgout"),)
    for ti, (o, w) in enumerate(_tiles(HALO, TOK)):
        ln_tile(ctx, H, f"snln.t{ti}", o, w,
                out_f32=[x[:, c, o:o + w] for c in range(NCHUNK)])
        H["_ln_cons"] = ()

    seen_par = set()
    seen_ff = [False]
    for s in (0, 1):
        sp = f"sn{s}"
        W = H["in"]["W0z"] if s == 0 else H["in"]["W1z"]
        if s == 0:
            ctx.wait("sc", "snln.t3.done")
        else:
            ctx.wait("sc", "sn0.corr")
        for c in range(NCHUNK):
            for tt in range(4):
                i = sc.copy(S["hnF"][:, c, 512 * tt:512 * (tt + 1)],
                            x[:, c, HALO + 512 * tt:HALO + 512 * (tt + 1)])
        ctx.inc("sc", i, "sc", tag=f"{sp}.hn")
        for c in range(NCHUNK):
            cp = f"{sp}.c{c}"
            j = c % 2
            if c >= 2:
                ctx.wait("gp", f"{sp}.c{c-2}.zmmlast")
            elif s == 1:
                ctx.wait("gp", f"sn0.c{6 + c}.zmmlast")
            for k3, off in ((0, 0), (1, 1024), (2, 2048)):
                i = gp.dma_start(out=S["Wz"][j][:, :, 128 * k3:128 * (k3 + 1)],
                                 in_=W[:, off + 128 * c:off + 128 * (c + 1)]
                                 .rearrange("(cc p) n -> p cc n", p=128))
                ctx.inc("gp", i, "da" if j == 0 else "db", 16)
            ctx.mark(f"{cp}.w", "da" if j == 0 else "db")
            ctx.wait("pe", f"{cp}.w")
            ctx.wait("pe", f"{sp}.hn")
            for t in range(4):
                tp = f"{cp}.t{t}"
                o = HALO + 512 * t
                par = t % 2
                zb = (4, 5, 3) if par == 0 else (0, 1, 2)
                if par in seen_par:
                    ctx.wait("pe", f"zfree{par}_ve")
                    ctx.wait("pe", f"zfree{par}_sc")
                for k3 in range(3):
                    for cc in range(NCHUNK):
                        i = pe.matmul(ps[zb[k3]][:, 0:512],
                                      S["Wz"][j][:, cc, 128 * k3:128 * (k3 + 1)],
                                      S["hnF"][:, cc, 512 * t:512 * (t + 1)],
                                      start=(cc == 0), stop=(cc == NCHUNK - 1))
                ctx.inc("pe", i, "pe", tag=f"{tp}.zmm")
                if t == 3:
                    ctx.marks[f"{cp}.zmmlast"] = ctx.marks[f"{tp}.zmm"]
                f = S["f"]
                ctx.wait("sc", f"{tp}.zmm")
                if seen_ff[0]:
                    ctx.wait("sc", "snffree")
                i = sc.activation(f[:, 0, 0:512], ps[zb[1]][:, 0:512], AF.Sigmoid,
                                  scale=-1.0)
                ctx.chain("sc", i)
                i = sc.activation(f[:, 1, 0:512], ps[zb[1]][:, 0:512], AF.Sigmoid)
                ctx.chain("sc", i)
                i = sc.activation(f[:, 2, 0:512], ps[zb[0]][:, 0:512], AF.Sigmoid)
                ctx.inc("sc", i, "sc", tag=f"{tp}.sig")
                ctx.wait("ve", f"{tp}.zmm")
                ctx.wait("ve", f"{tp}.sig")
                i = ve.scalar_tensor_tensor(f[:, 3, 0:512], ps[zb[0]][:, 0:512],
                                            0.5, f[:, 2, 0:512],
                                            ALU.add, ALU.subtract)
                ctx.inc("ve", i, "ve", tag=f"{tp}.gt")
                ctx.mark(f"{tp}.zdone_hg", "ve")
                ctx.wait("sc", f"{tp}.gt")
                i = sc.activation(f[:, 5, 0:512], f[:, 3, 0:512], AF.Relu)
                ctx.chain("sc", i)
                i = sc.activation(f[:, 4, 0:512], ps[zb[2]][:, 0:512], AF.Sigmoid)
                ctx.inc("sc", i, "sc", tag=f"{tp}.acts")
                ctx.wait("ve", f"{tp}.acts")
                i = ve.tensor_tensor(f[:, 5, 0:512], f[:, 2, 0:512],
                                     f[:, 5, 0:512], ALU.add)
                ctx.chain("ve", i)
                i = ve.tensor_tensor(f[:, 6, 0:512], f[:, 1, 0:512],
                                     f[:, 5, 0:512], ALU.mult)
                ctx.chain("ve", i)
                i = ve.tensor_tensor_scan(
                    f[:, 7, 0:512], f[:, 0, 0:512], f[:, 6, 0:512],
                    0.0 if t == 0 else S["carH"][:, c:c + 1], ALU.mult, ALU.add)
                ctx.chain("ve", i)
                i = ve.tensor_copy(S["carH"][:, c:c + 1], f[:, 7, 511:512])
                ctx.chain("ve", i)
                i = ve.tensor_tensor_scan(
                    f[:, 2, 0:512], f[:, 0, 0:512], S["zeros512"][:, 0:512],
                    1.0 if t == 0 else S["carA"][:, c:c + 1], ALU.mult, ALU.add)
                ctx.chain("ve", i)
                i = ve.tensor_copy(S["carA"][:, c:c + 1], f[:, 2, 511:512])
                ctx.chain("ve", i)
                i = ve.tensor_tensor(S["gA"][:, c, 512 * t:512 * (t + 1)],
                                     f[:, 4, 0:512], f[:, 2, 0:512], ALU.mult)
                ctx.chain("ve", i)
                hsl = x[:, c, o:o + 512]
                i = ve.tensor_tensor(f[:, 6, 0:512], f[:, 7, 0:512], hsl,
                                     ALU.subtract)
                ctx.chain("ve", i)
                i = ve.tensor_tensor(f[:, 3, 0:512], f[:, 4, 0:512],
                                     f[:, 6, 0:512], ALU.mult)
                ctx.chain("ve", i)
                i = ve.tensor_tensor(hsl, hsl, f[:, 3, 0:512], ALU.add)
                ctx.inc("ve", i, "ve", tag=f"{tp}.hupd")
                ctx.mark("snffree", "ve")
                seen_ff[0] = True
                ctx.marks[f"zfree{par}_ve"] = ctx.marks[f"{tp}.zdone_hg"]
                ctx.marks[f"zfree{par}_sc"] = ctx.marks[f"{tp}.acts"]
                seen_par.add(par)
        ctx.mark(f"{sp}.localdone", "ve")
        shm_c, shm_f = H["shm_c"], H["shm_f"]
        ctx.wait("gp", f"{sp}.localdone")
        if ctx.live == "gp":
            e = ctx.eng_handles["gp"]
            with (e.register(f"myh{s}") as myh, e.register(f"oth{s}") as oth,
                  e.register(f"off{s}") as off, e.register(f"want{s}") as want,
                  e.register(f"fv{s}") as fv, e.register(f"diff{s}") as diff,
                  e.register(f"cnt{s}") as cnt):
                e.reg_load(myh, H["hxt"][0:1, 0:1])
                e.reg_mov(oth, 1)
                e.reg_sub(oth, oth, myh)
                e.reg_mov(off, 2048)
                e.reg_mul(off, off, myh)
                e.reg_add(off, off, 1024 * s)
                e.dma_start(out=bass.AP(shm_c, off, [[8, 128], [1, 8]]),
                            in_=S["carH"][:]).then_inc(ctx.sems["dw"], 16)
                e.wait_ge(ctx.sems["dw"], ctx.cnt["dw"] + 16)
                e.reg_mov(off, 2)
                e.reg_mul(off, off, myh)
                e.reg_add(off, off, s)
                e.dma_start(out=bass.AP(shm_f, off, [[1, 1], [1, 1]]),
                            in_=H["noncet"][0:1, s:s + 1]).then_inc(
                                ctx.sems["dw"], 16)
                e.reg_load(want, H["noncet"][0:1, s:s + 1])
                e.reg_mov(off, 2)
                e.reg_mul(off, off, oth)
                e.reg_add(off, off, s)
                e.reg_mov(cnt, 0)
                e.dma_start(out=S["flag"][:],
                            in_=bass.AP(shm_f, off, [[1, 1], [1, 1]])).then_inc(
                                ctx.sems["pl"], 16)
                e.reg_add(cnt, cnt, 16)
                e.wait_ge(ctx.sems["pl"], cnt)
                e.reg_load(fv, S["flag"][0:1, 0:1])
                e.reg_sub(diff, fv, want)
                with e.While(diff):
                    e.dma_start(out=S["flag"][:],
                                in_=bass.AP(shm_f, off, [[1, 1], [1, 1]])).then_inc(
                                    ctx.sems["pl"], 16)
                    e.reg_add(cnt, cnt, 16)
                    e.wait_ge(ctx.sems["pl"], cnt)
                    e.reg_load(fv, S["flag"][0:1, 0:1])
                    e.reg_sub(diff, fv, want)
                e.reg_mov(off, 2048)
                e.reg_mul(off, off, oth)
                e.reg_add(off, off, 1024 * s)
                e.dma_start(out=S["Hin"][:],
                            in_=bass.AP(shm_c, off, [[8, 128], [1, 8]])).then_inc(
                                ctx.sems["dw"], 16)
        ctx.cnt["dw"] += 48
        if ctx.live == "gp":
            ctx.emitted["dw"] = ctx.emitted.get("dw", 0) + 48
        ctx.mark(f"{sp}.hin", "dw")
        ctx.wait("ve", f"{sp}.hin")
        ctx.wait("ve", f"{sp}.localdone")
        i = ve.tensor_scalar_mul(S["Hin"][:], S["Hin"][:], H["cg"][:, 0:1])
        ctx.chain("ve", i)
        for c in range(NCHUNK):
            for t in range(4):
                i = ve.scalar_tensor_tensor(
                    x[:, c, HALO + 512 * t:HALO + 512 * (t + 1)],
                    S["gA"][:, c, 512 * t:512 * (t + 1)], S["Hin"][:, c:c + 1],
                    x[:, c, HALO + 512 * t:HALO + 512 * (t + 1)],
                    ALU.mult, ALU.add)
        ctx.inc("ve", i, "ve", tag=f"{sp}.corr")
    ctx.wait("sy", "sn1.corr")
    ctx.wait("ve", "sn1.corr")
    for c in range(NCHUNK):
        if c >= 1:
            ctx.wait("sy", f"scan.xr{c-1}")
        i = sy.dma_start(out=S["xst"][:], in_=xpage[128 * c:128 * (c + 1), :])
        ctx.inc("sy", i, "dx", 16, tag=f"scan.xld{c}")
        ctx.wait("ve", f"scan.xld{c}")
        i = ve.tensor_tensor(x[:, c, HALO:TOK], x[:, c, HALO:TOK],
                             S["xst"][:], ALU.add)
        ctx.inc("ve", i, "ve", tag=f"scan.xr{c}")
    H["_ln_cons"] = (("ve", f"scan.xr{NCHUNK-1}"),)



_BUILD_CACHE = {}


def _get_program(phases):
    key = tuple(phases)
    if key not in _BUILD_CACHE:
        _BUILD_CACHE[key] = build_program(list(phases))
    return _BUILD_CACHE[key]


def _prep_static(inputs):
    f32 = np.float32
    asc = np.ascontiguousarray
    g = {}
    g["qT"] = asc(inputs["L0_attn_q_w"].T.astype(f32))
    g["kT"] = asc(inputs["L0_attn_k_w"].T.astype(f32))
    g["vW"] = asc(inputs["L0_attn_v_w"].T.astype(f32))
    g["oT"] = asc(inputs["L0_attn_o_w"].T.astype(f32))
    for nm, src in (("qb", "L0_attn_q_b"), ("kb", "L0_attn_k_b"),
                    ("vb", "L0_attn_v_b"), ("ob", "L0_attn_o_b")):
        g[nm] = asc(inputs[src].astype(f32).reshape(1, -1))
    g["Win"] = asc(inputs["L0_ffn_Win"].astype(f32))
    g["binr"] = asc(inputs["L0_ffn_bin_"].astype(f32).reshape(1, -1))
    Emat = np.zeros((64, 1024), f32)
    for m in range(64):
        Emat[m, m * 16:(m + 1) * 16] = 1.0
    g["E"] = Emat
    g["W1b"] = asc(inputs["L0_ffn_W1"].astype(f32).reshape(-1).reshape(8, 128).T)
    g["b1b"] = asc(inputs["L0_ffn_b1"].astype(f32).reshape(-1).reshape(8, 128).T)
    scat = np.zeros((1024, 64), f32)
    W2 = inputs["L0_ffn_W2"].astype(f32)
    for m in range(64):
        scat[m * 16:(m + 1) * 16, m] = W2[m]
    g["scat"] = scat
    g["b2r"] = asc(inputs["L0_ffn_b2"].astype(f32).reshape(1, -1))
    g["Wout"] = asc(inputs["L0_ffn_Wout"].astype(f32))
    g["boutr"] = asc(inputs["L0_ffn_bout"].astype(f32).reshape(1, -1))
    g["w1T"] = asc(inputs["L1_ffn_w1"].astype(f32).T)
    g["b1r"] = asc(inputs["L1_ffn_b1"].astype(f32).reshape(1, -1))
    g["w2T"] = asc(inputs["L1_ffn_w2"].astype(f32).T)
    g["b2r1"] = asc(inputs["L1_ffn_b2"].astype(f32).reshape(1, -1))
    g["W0z"] = asc(inputs["L1_attn_W0"].astype(f32))
    g["W1z"] = asc(inputs["L1_attn_W1"].astype(f32))
    g["ident"] = np.eye(128, dtype=ml_dtypes.bfloat16)

    r = np.arange(128)[:, None]
    c = np.arange(384)[None, :]
    band = ((c - r >= 1) & (c - r <= 256)).astype(f32)
    def mk(extra_lo):
        m = band * (c >= extra_lo)
        m = np.concatenate([m[:, 128 * b:128 * (b + 1)].T for b in range(3)],
                           axis=1)
        return np.ascontiguousarray(m).astype(ml_dtypes.bfloat16)
    g["maskC"] = mk(-1)
    g["_maskA_start"] = mk(256)
    g["_maskB_start"] = mk(128)

    import jax
    with jax.default_device(jax.devices("cpu")[0]):
        keep = np.asarray(jax.random.bernoulli(
            jax.random.key(42, impl="threefry2x32"), 0.8, (B, T, D)))
    g["_keep"] = keep
    return g


def _core_inputs(g, inputs, core, nonce):
    b, h = core // 2, core % 2
    x = inputs["L0_attn_x"].astype(np.float32)
    lo = h * HALF - HALO
    if lo < 0:
        shard = np.concatenate(
            [np.zeros((HALO, D), np.float32), x[b, 0:HALF]], axis=0)
    else:
        shard = x[b, lo:lo + TOK]
    d = {k: v for k, v in g.items() if not k.startswith("_")}
    d["xT"] = np.ascontiguousarray(shard.T)
    if h == 0:
        d["maskA"], d["maskB"] = g["_maskA_start"], g["_maskB_start"]
        d["cg"] = np.zeros((128, 1), np.float32)
    else:
        d["maskA"], d["maskB"] = g["maskC"], g["maskC"]
        d["cg"] = np.ones((128, 1), np.float32)
    keep = g["_keep"][b, h * HALF:(h + 1) * HALF]
    dm = (keep.astype(np.float32) * 0.625).T
    d["dmask"] = np.ascontiguousarray(dm.astype(ml_dtypes.bfloat16))
    d["nonce"] = np.array([[nonce, nonce + 1]], np.int32)
    d["halfidx"] = np.array([[h]], np.int32)
    return d


FUSED_PHASES = ("init", "f0a", "att", "f0b", "f1a", "scan", "f1b", "out")

PHASE_GROUPS = [FUSED_PHASES]


def kernel(**inputs):
    g = _prep_static(inputs)
    x = inputs["L0_attn_x"].astype(np.float32)
    out = np.empty((B, T, D), np.float32)
    cur = x
    for gi, group in enumerate(PHASE_GROUPS):
        phases = tuple(group)
        nc = _get_program(phases)
        nonce = int(np.random.randint(1, 2**30)) + gi
        ins_full = dict(inputs)
        ins_full["L0_attn_x"] = cur
        maps = [_core_inputs(g, ins_full, c, nonce) for c in range(8)]
        res = run_bass_kernel_spmd(nc, maps, core_ids=list(range(8)))
        nxt = np.empty_like(cur)
        for c in range(8):
            b, h = c // 2, c % 2
            nxt[b, h * HALF:(h + 1) * HALF] = res.results[c]["y"].T
        cur = nxt
    out[:] = cur
    return out


def phase_attn(ctx, H):
    sy, gp, pe, sc, ve = (ctx.eng(e) for e in ENGS)
    ps = H["ps"]
    A = H["att"]
    tiles = _tiles(0, TOK)

    for nm in ("maskA", "maskB", "maskC"):
        i = sy.dma_start(out=A[nm][:], in_=H["in"][nm])
        ctx.inc("sy", i, "dx", 16, tag="att.cst" if nm == "maskC" else None)
    i = ve.memset(A["vTbuf"][:, :, :, 64:65], 1.0)
    ctx.inc("ve", i, "ve")
    for e in ("pe", "ve", "sc"):
        ctx.wait(e, "init.consts")
        ctx.wait(e, "init.x")
    for ti, (o, w) in enumerate(_tiles(0, TOK)):
        ln_tile(ctx, H, f"attln.t{ti}", o, w, out_bf=A["xnf"], out_off=o)
    for half in (0, 1):
        hp = f"att{half}"
        ctx.wait("pe", "att.cst")
        if half == 1:
            ctx.wait("gp", "att0.lastq")
            ctx.wait("gp", "att0.lastprod")
        for nm, dst in (("qT", A["qTh"]), ("kT", A["kTh"]), ("vW", A["vWh"])):
            i = gp.dma_start(out=dst[:], in_=H["in"][nm][:, 512 * half:512 * (half + 1)]
                             .rearrange("(c p) n -> p c n", p=128))
            ctx.inc("gp", i, "dw", 16)
        i = gp.dma_start(out=A["oTh"][:],
                         in_=H["in"]["oT"][512 * half:512 * (half + 1), :]
                         .rearrange("(c p) n -> p c n", p=128))
        ctx.inc("gp", i, "dw", 16)
        for nm, off in (("qb", 0), ("kb", 512), ("vb", 1024)):
            i = gp.dma_start(out=A["rowsA"][0:1, off:off + 512],
                             in_=H["in"][nm][0:1, 512 * half:512 * half + 512])
            ctx.inc("gp", i, "dw", 16)
        i = gp.dma_start(out=A["rowsA"][0:1, 1536:2560], in_=H["in"]["ob"])
        ctx.inc("gp", i, "dw", 16, tag=f"{hp}.w")
        for e in ("pe", "ve", "sc"):
            ctx.wait(e, f"{hp}.w")

        prev_qp = "att0.t4.j15" if half == 1 else None
        prev2_qp = "att0.t4.j14" if half == 1 else None
        pending = [None]
        prev_op = ["att0.t4.j15" if half == 1 else None]
        for e in ("pe", "ve", "sc"):
            ctx.wait(e, "attln.t4.done")
        for ti, (o, w) in enumerate(tiles):
            ph = f"{hp}.t{ti}"
            lastj = None if prev_qp is None else prev_qp
            nkb = w // 128
            kb0 = o // 128
            for oc4 in range(4):
                bank = ps[4]
                if oc4 == 0 and lastj is not None:
                    ctx.wait("pe", f"{lastj}.exp5")
                if oc4 >= 1:
                    ctx.wait("pe", f"{ph}.q{oc4-1}.ev")
                for c in range(NCHUNK):
                    pe.matmul(bank[:, 0:w], A["qTh"][:, c, 128 * oc4:128 * (oc4 + 1)],
                              A["xnf"][:, c, o:o + w], start=(c == 0), stop=False)
                i = pe.matmul(bank[:, 0:w], A["rowsA"][0:1, 128 * oc4:128 * (oc4 + 1)],
                              H["onesrB"][0:1, 0:w], start=False, stop=True)
                ctx.inc("pe", i, "pe", tag=f"{ph}.q{oc4}.mm")
                ctx.wait("sc", f"{ph}.q{oc4}.mm")
                if oc4 == 0 and lastj is not None:
                    ctx.wait("sc", f"{lastj}.smm7")
                i = sc.mul(A["qbuf"][:, oc4, 0:w], bank[:, 0:w], 0.125)
                ctx.inc("sc", i, "sc", tag=f"{ph}.q{oc4}.ev")
            for oc4 in range(4):
                bank = ps[4]
                ctx.wait("pe", f"{ph}.q3.ev" if oc4 == 0 else f"{ph}.k{oc4-1}.ev")
                for c in range(NCHUNK):
                    pe.matmul(bank[:, 0:w], A["kTh"][:, c, 128 * oc4:128 * (oc4 + 1)],
                              A["xnf"][:, c, o:o + w], start=(c == 0), stop=False)
                i = pe.matmul(bank[:, 0:w],
                              A["rowsA"][0:1, 512 + 128 * oc4:512 + 128 * (oc4 + 1)],
                              H["onesrB"][0:1, 0:w], start=False, stop=True)
                ctx.inc("pe", i, "pe", tag=f"{ph}.k{oc4}.mm")
                ctx.wait("sc", f"{ph}.k{oc4}.mm")
                for j in range(nkb):
                    sl = (kb0 + j) % 6
                    i = sc.copy(A["kbuf"][:, oc4, 128 * sl:128 * (sl + 1)],
                                bank[:, 128 * j:128 * (j + 1)])
                ctx.inc("sc", i, "sc", tag=f"{ph}.k{oc4}.ev")
            for j in range(nkb):
                bank = ps[4]
                ctx.wait("pe", f"{ph}.k3.ev" if j == 0 else f"{ph}.v{j-1}.ev")
                for c in range(NCHUNK):
                    pe.matmul(bank[:, 0:512], A["xnf"][:, c, o + 128 * j:o + 128 * (j + 1)],
                              A["vWh"][:, c, 0:512], start=(c == 0), stop=False)
                i = pe.matmul(bank[:, 0:512], H["onesc1B"][0:1, 0:128],
                              A["rowsA"][0:1, 1024:1536], start=False, stop=True)
                ctx.inc("pe", i, "pe", tag=f"{ph}.v{j}.mm")
                ctx.wait("sc", f"{ph}.v{j}.mm")
                if j == 0 and lastj is not None:
                    ctx.wait("sc", f"{lastj}.av7")
                sl = (kb0 + j) % 6
                i = sc.copy(A["vTbuf"][:, sl, :, 0:64],
                            bank[:, 0:512].rearrange("p (h n) -> p h n", h=8))
                ctx.inc("sc", i, "sc", tag=f"{ph}.v{j}.ev")
            ctx.mark(f"{ph}.prodmm", "ve")
            def emit_oproj(qp_, qj_, qlo_, qt_):
                bank = ps[2]
                for grp in range(2):
                    ctx.wait("pe", f"{qp_}.ot7")
                    if grp == 1:
                        ctx.wait("pe", f"{qp_}.og0")
                    elif prev_op[0] is not None:
                        ctx.wait("pe", f"{prev_op[0]}.og1")
                        prev_op[0] = None
                    for oc4i in range(4):
                        oc = 4 * grp + oc4i
                        for c4 in range(4):
                            i = pe.matmul(bank[:, 128 * oc4i:128 * (oc4i + 1)],
                                          A["oTh"][:, c4, 128 * oc:128 * (oc + 1)],
                                          A["o_in"][:, qj_ % 2, c4, 0:128],
                                          start=(c4 == 0),
                                          stop=(c4 == 3 and half == 1))
                        if half == 0:
                            i = pe.matmul(bank[:, 128 * oc4i:128 * (oc4i + 1)],
                                          A["rowsA"][0:1, 1536 + 128 * oc:1536 + 128 * (oc + 1)],
                                          H["onesrB"][0:1, 0:128],
                                          start=False, stop=True)
                    ctx.inc("pe", i, "pe", tag=f"{qp_}.omm{grp}")
                    ctx.wait("ve", f"{qp_}.omm{grp}")
                    xs = H["x"][:, 4 * grp:4 * grp + 4,
                                qlo_ + 128 * qt_:qlo_ + 128 * (qt_ + 1)]
                    i = ve.scalar_tensor_tensor(
                        xs, bank[:, 0:512].rearrange("p (c n) -> p c n", c=4),
                        1.0, xs, ALU.mult, ALU.add)
                    ctx.inc("ve", i, "ve", tag=f"{qp_}.og{grp}")
                prev_op[0] = qp_
            qlo = max(o, HALO)
            nq = (o + w - qlo) // 128
            for qt in range(nq):
                qj = (qlo - HALO) // 128 + qt
                qoff = qlo - o + 128 * qt
                qp = f"{ph}.j{qj}"
                mask = (A["maskA"] if qj == 0 else
                        A["maskB"] if qj == 1 else A["maskC"])
                pq = prev_qp
                ppq = prev2_qp
                SBK = (0, 1, 4)
                def emit_smm(hh):
                    sb = ps[SBK[hh % 3]]
                    lo64 = 64 * (hh % 2)
                    ch = hh // 2
                    if hh >= 3:
                        ctx.wait("pe", f"{qp}.exp{hh-3}")
                    else:
                        if hh == 2 and qt == 0:
                            ctx.wait("pe", f"{ph}.v{nkb-1}.ev")
                        elif pq is not None:
                            ctx.wait("pe", f"{pq}.exp{ {0: 6, 1: 7, 2: 5}[hh] }")
                    qsl = A["qbuf"][lo64:lo64 + 64, ch, qoff:qoff + 128]
                    for b3 in range(3):
                        sl = (qj + b3) % 6
                        i = pe.matmul(sb[:, 128 * b3:128 * (b3 + 1)],
                                      A["kbuf"][lo64:lo64 + 64, ch, 128 * sl:128 * (sl + 1)],
                                      qsl, start=True, stop=True)
                    ctx.inc("pe", i, "pe", tag=f"{qp}.smm{hh}")
                def emit_exp(hh):
                    sb = ps[SBK[hh % 3]]
                    ctx.wait("sc", f"{qp}.smm{hh}")
                    i = sc.activation(A["expA"][:, hh, 0:384], sb[:, 0:384],
                                      AF.Exp, bias=H["neg8"][:, 0:1])
                    ctx.inc("sc", i, "sc", tag=f"{qp}.exp{hh}")
                def emit_mt(hh):
                    ctx.wait("ve", f"{qp}.exp{hh}")
                    i = ve.tensor_tensor(A["expA"][:, hh, 0:384],
                                         A["expA"][:, hh, 0:384],
                                         mask[:], ALU.mult)
                    ctx.inc("ve", i, "ve", tag=f"{qp}.mt{hh}")
                def emit_av(hh):
                    ab = ps[3 if hh % 2 == 0 else 5]
                    ctx.wait("pe", f"{qp}.mt{hh}")
                    if hh >= 2:
                        ctx.wait("pe", f"{qp}.orv{hh-2}")
                    elif pq is not None:
                        ctx.wait("pe", f"{pq}.orv{6 + hh}")
                    for b3 in range(3):
                        sl = (qj + b3) % 6
                        i = pe.matmul(ab[0:65, 0:128],
                                      A["vTbuf"][:, sl, hh, 0:65],
                                      A["expA"][:, hh, 128 * b3:128 * (b3 + 1)],
                                      start=(b3 == 0), stop=(b3 == 2))
                    ctx.inc("pe", i, "pe", tag=f"{qp}.av{hh}")
                def emit_orv(hh):
                    ab = ps[3 if hh % 2 == 0 else 5]
                    ctx.wait("sc", f"{qp}.av{hh}")
                    if hh >= 4:
                        ctx.wait("sc", f"{qp}.ot{hh-4}")
                    elif pq is not None:
                        ctx.wait("sc", f"{pq}.ot{4 + hh}")
                    i = sc.copy(A["oraw"][0:65, hh % 4, 0:128], ab[0:65, 0:128])
                    ctx.inc("sc", i, "sc", tag=f"{qp}.orv{hh}")
                def emit_rc(hh):
                    ctx.wait("sc", f"{qp}.orv{hh}")
                    i = sc.activation(A["lnw"][0:1, hh % 2, 0:128],
                                      A["oraw"][64:65, hh % 4, 0:128], AF.Ln)
                    ctx.chain("sc", i)
                    if hh >= 2:
                        ctx.wait("sc", f"{qp}.bc{hh-2}")
                    elif pq is not None:
                        ctx.wait("sc", f"{pq}.bc{6 + hh}")
                    with ctx.nc.allow_low_precision(
                            reason="softmax 1/sum broadcast via bf16 matmul"):
                        i = sc.activation(A["rrow"][0:1, hh % 2, 0:128],
                                          A["lnw"][0:1, hh % 2, 0:128],
                                          AF.Exp, scale=-1.0)
                    ctx.inc("sc", i, "sc", tag=f"{qp}.rc{hh}")
                def emit_bc(hh):
                    bb = ps[6 if hh % 2 == 0 else 7]
                    ctx.wait("pe", f"{qp}.rc{hh}")
                    if hh >= 2:
                        ctx.wait("pe", f"{qp}.ot{hh-2}")
                    elif pq is not None:
                        ctx.wait("pe", f"{pq}.ot{6 + hh}")
                    i = pe.matmul(bb[0:64, 0:128], H["onesc1B"][0:1, 0:64],
                                  A["rrow"][0:1, hh % 2, 0:128],
                                  start=True, stop=True)
                    ctx.inc("pe", i, "pe", tag=f"{qp}.bc{hh}")
                def emit_ot(hh):
                    bb = ps[6 if hh % 2 == 0 else 7]
                    lo64 = 64 * (hh % 2)
                    ch = hh // 2
                    ctx.wait("ve", f"{qp}.bc{hh}")
                    if hh == 0 and ppq is not None:
                        ctx.wait("ve", f"{ppq}.omm1")
                    i = ve.tensor_tensor(A["o_in"][lo64:lo64 + 64, qj % 2, ch, 0:128],
                                         bb[0:64, 0:128],
                                         A["oraw"][0:64, hh % 4, 0:128], ALU.mult)
                    ctx.inc("ve", i, "ve", tag=f"{qp}.ot{hh}")
                for hh in range(11):
                    if hh == 2 and pending[0] is not None:
                        emit_oproj(*pending[0])
                        pending[0] = None
                    if hh < 8:
                        emit_smm(hh)
                        emit_exp(hh)
                        emit_mt(hh)
                    if 2 <= hh < 10:
                        emit_av(hh - 2)
                        emit_orv(hh - 2)
                        emit_rc(hh - 2)
                    if hh >= 3:
                        emit_bc(hh - 3)
                        emit_ot(hh - 3)
                pending[0] = (qp, qj, qlo, qt)
                prev2_qp = prev_qp
                prev_qp = qp
                prev_qp = qp
            pass
        if pending[0] is not None:
            emit_oproj(*pending[0])
            pending[0] = None
        ctx.marks[f"{hp}.lastq"] = ctx.marks[f"{hp}.t4.j15.og1"]
        ctx.marks[f"{hp}.lastprod"] = ctx.marks[f"{hp}.t4.v1.ev"]
    H["_ln_cons"] = (("ve", "att1.lastq"), ("sc", "att1.t4.j15.exp7"),
                     ("pe", "att1.t4.j15.omm1"))


def phase_l1ffn(ctx, H, which):
    sy, gp, pe, sc, ve = (ctx.eng(e) for e in ENGS)
    ps = H["ps"]
    F = H["f1"]
    i = gp.dma_start(out=F["w1q"][:],
                     in_=H["in"]["w1T"].rearrange("(c p) n -> p c n", p=128))
    ctx.inc("gp", i, "dw", 16)
    i = gp.dma_start(out=F["w2q"][:],
                     in_=H["in"]["w2T"].rearrange("(c p) n -> p c n", p=128))
    ctx.inc("gp", i, "dw", 16)
    i = gp.dma_start(out=F["rf8"][0:1, 0:4096], in_=H["in"]["b1r"])
    ctx.inc("gp", i, "dw", 16)
    i = gp.dma_start(out=F["rf8"][0:1, 4096:5120], in_=H["in"]["b2r1"])
    ctx.inc("gp", i, "dw", 16, tag=f"{which}.w")
    for e in ("pe", "ve", "sc"):
        ctx.wait(e, f"{which}.w")
        ctx.wait(e, "init.consts")

    ptp = None
    for ti in range(4):
        o = HALO + 512 * ti
        tp = f"{which}.t{ti}"
        ln_tile(ctx, H, f"{tp}.ln", o, 512, out_bf=F["xq"])
        H["_ln_cons"] = (("ve", f"{tp}.hmm31"),)
        if ptp is not None:
            ctx.wait("sy", f"{ptp}.y7")
        i = sy.dma_start(out=F["dm"][:],
                         in_=H["in"]["dmask"][:, 512 * ti:512 * (ti + 1)]
                         .rearrange("(c p) n -> p c n", p=128))
        ctx.inc("sy", i, "dx", 16, tag=f"{tp}.dm")
        ctx.wait("pe", f"{tp}.ln.done")
        for hc in range(32):
            bank = ps[4 + (hc % 2)]
            if hc >= 2:
                ctx.wait("pe", f"{tp}.h16_{hc-2}")
            elif ptp is not None:
                ctx.wait("pe", f"{ptp}.h16_{30 + hc}")
            for c in range(NCHUNK):
                pe.matmul(bank[:, 0:512], F["w1q"][:, c, 128 * hc:128 * (hc + 1)],
                          F["xq"][:, c, 0:512], start=(c == 0), stop=False)
            i = pe.matmul(bank[:, 0:512],
                          F["rf8"][0:1, 128 * hc:128 * (hc + 1)],
                          H["onesrF8"][0:1, 0:512], start=False, stop=True)
            ctx.inc("pe", i, "pe", tag=f"{tp}.hmm{hc}")
            ctx.wait("sc", f"{tp}.hmm{hc}")
            if hc >= 1:
                ctx.wait("sc", f"{tp}.hq{hc-1}")
            i = sc.activation(F["h16"][:, 0:512], bank[:, 0:512], AF.Relu)
            ctx.inc("sc", i, "sc", tag=f"{tp}.h16_{hc}")
            ctx.wait("ve", f"{tp}.h16_{hc}")
            if hc == 0 and ptp is not None:
                ctx.wait("ve", f"{ptp}.ymm7")
            i = ve.tensor_copy(F["hq"][:, hc, 0:512], F["h16"][:, 0:512])
            ctx.inc("ve", i, "ve", tag=f"{tp}.hq{hc}")
        for oc in range(NCHUNK):
            bank = ps[2 + (oc % 2)]
            ctx.wait("pe", f"{tp}.hq31")
            if oc >= 2:
                ctx.wait("pe", f"{tp}.y16_{oc-2}")
            elif ptp is not None:
                ctx.wait("pe", f"{ptp}.y16_{6 + oc}")
            for hc in range(32):
                pe.matmul(bank[:, 0:512], F["w2q"][:, hc, 128 * oc:128 * (oc + 1)],
                          F["hq"][:, hc, 0:512], start=(hc == 0), stop=False)
            i = pe.matmul(bank[:, 0:512],
                          F["rf8"][0:1, 4096 + 128 * oc:4096 + 128 * (oc + 1)],
                          H["onesrF8"][0:1, 0:512], start=False, stop=True)
            ctx.inc("pe", i, "pe", tag=f"{tp}.ymm{oc}")
            ctx.wait("sc", f"{tp}.ymm{oc}")
            if oc >= 1:
                ctx.wait("sc", f"{tp}.y{oc-1}")
            elif ptp is not None:
                ctx.wait("sc", f"{ptp}.y7")
            i = sc.copy(F["y16"][:, 0:512], bank[:, 0:512])
            ctx.inc("sc", i, "sc", tag=f"{tp}.y16_{oc}")
            ctx.wait("ve", f"{tp}.y16_{oc}")
            ctx.wait("ve", f"{tp}.dm")
            i = ve.tensor_tensor(H["stF"][0:, 0:512], F["y16"][:, 0:512],
                                 F["dm"][:, oc, 0:512], ALU.mult)
            ctx.chain("ve", i)
            i = ve.tensor_tensor(H["x"][:, oc, o:o + 512],
                                 H["x"][:, oc, o:o + 512],
                                 H["stF"][0:, 0:512], ALU.add)
            ctx.inc("ve", i, "ve", tag=f"{tp}.y{oc}")
        H["_ln_cons"] = (("ve", f"{tp}.hmm31"),
                         ("pe", f"{tp}.y16_6"), ("pe", f"{tp}.y16_7"),
                         ("pe", f"{tp}.y7"))
        ptp = tp


# version 15
# speedup vs baseline: 1.9138x; 1.9138x over previous
import contextlib
import numpy as np
import ml_dtypes
import concourse.bass as bass
import concourse.mybir as mybir
from concourse.bass_utils import run_bass_kernel_spmd

F32 = mybir.dt.float32
BF16 = mybir.dt.bfloat16
FP16 = mybir.dt.float16
F8E4 = mybir.dt.float8e4
I32 = mybir.dt.int32
U8 = mybir.dt.uint8
AF = mybir.ActivationFunctionType
ALU = mybir.AluOpType
AX = mybir.AxisListType

B, T, D = 4, 4096, 1024
HALF = 2048
HALO = 256
TOK = HALF + HALO
NCHUNK = 8
EPS = 1e-5

ENGS = ("sy", "gp", "pe", "sc", "ve")
SEMS = ("dx", "dw", "da", "db", "pe", "sc", "ve", "gp")


def _tiles(lo, hi, w=512):
    out = []
    o = lo
    while o < hi:
        out.append((o, min(w, hi - o)))
        o += w
    return out


class _DummyInstr:
    def then_inc(self, *a, **k):
        return self


class _DummyAny:
    def __call__(self, *a, **k):
        return self
    def __getattr__(self, name):
        return self
    def __enter__(self):
        return self
    def __exit__(self, *a):
        return False


class Ctx:

    def __init__(self, nc, sems):
        self.nc = nc
        self.sems = sems
        self.cnt = {s: 0 for s in SEMS}
        self.marks = {}
        self.live = None
        self.eng_handles = {}

    def begin_pass(self, live, cnt0):
        self.live = live
        self.cnt = dict(cnt0)

    def eng(self, name):
        if self.live == name:
            return self.eng_handles[name]
        return _DummyAny()

    def inc(self, name, instr, sem, n=1, tag=None):
        self.cnt[sem] += n
        if self.live == name:
            assert not isinstance(instr, _DummyAny), \
                f"inc({name},{sem},{tag}): dummy instruction in live pass ({tag})"
            instr.then_inc(self.sems[sem], n)
            self.emitted[sem] = self.emitted.get(sem, 0) + n
        if tag is not None:
            self.marks[tag] = (sem, self.cnt[sem])
        return self.cnt[sem]

    def mark(self, tag, sem):
        self.marks[tag] = (sem, self.cnt[sem])

    def wait(self, name, tag):
        if tag not in self.marks:
            assert self.live is None, f"unknown mark {tag} in emit pass"
            return
        sem, v = self.marks[tag]
        if sem in ("dx", "dw"):
            v = self.cnt[sem]
        if self.live == name:
            self.eng_handles[name].wait_ge(self.sems[sem], v)

    def chain(self, name, instr):
        sem = {"sy": "dx", "gp": "gp", "pe": "pe", "sc": "sc", "ve": "ve"}[name]
        v = self.inc(name, instr, sem)
        if self.live == name:
            self.eng_handles[name].wait_ge(self.sems[sem], v)

    def wait_all(self, name):
        if self.live == name:
            e = self.eng_handles[name]
            for s in SEMS:
                if self.cnt[s] > 0:
                    e.wait_ge(self.sems[s], self.cnt[s])



def ln_tile(ctx, H, ph, o, w, out_bf=None, out_f32=None, out_off=0):
    sy, gp, pe, sc, ve = (ctx.eng(e) for e in ENGS)
    x = H["x"]
    xb, sq = H["xb"], H["sq"]
    rows = H["rows"]
    psm, psv, psrB, psmrB = H["ps"][0], H["ps"][1], H["ps"][2], H["ps"][3]
    prev = H.get("_ln_prev")
    if prev is not None:
        ctx.wait("ve", f"{prev}.stats")
        ctx.wait("sc", f"{prev}.stats")
        ctx.wait("pe", f"{prev}.rows")
        ctx.wait("pe", f"{prev}.m2")
        ctx.wait("sc", f"{prev}.var")
        ctx.wait("ve", f"{prev}.bcast")
        ctx.wait("pe", f"{prev}.done")
    for eng_t, t in H.get("_ln_cons", ()):
        ctx.wait(eng_t, t)
    H["_ln_cons"] = ()
    H["_ln_prev"] = ph

    for c in range(NCHUNK):
        i = ve.tensor_copy(xb[:, c, 0:w], x[:, c, o:o + w])
    ctx.inc("ve", i, "ve", tag=f"{ph}.xbdone")
    for c in range(NCHUNK):
        i = sc.activation(sq[:, c, 0:w], x[:, c, o:o + w], AF.Square)
    ctx.inc("sc", i, "sc", tag=f"{ph}.sq")
    ctx.wait("pe", f"{ph}.xbdone")
    ctx.wait("pe", f"{ph}.sq")
    for c in range(NCHUNK):
        pe.matmul(psm[0:1, 0:w], H["statcol"][:], xb[:, c, 0:w],
                  start=(c == 0), stop=(c == NCHUNK - 1))
    for c in range(NCHUNK):
        i = pe.matmul(psv[0:1, 0:w], H["statcol"][:], sq[:, c, 0:w],
                      start=(c == 0), stop=(c == NCHUNK - 1))
    ctx.inc("pe", i, "pe", tag=f"{ph}.stats")
    ctx.wait("sc", f"{ph}.stats")
    i = sc.activation(rows[0:1, 0:w], psm[0:1, 0:w], AF.Square)
    ctx.inc("sc", i, "sc", tag=f"{ph}.m2")
    ctx.wait("ve", f"{ph}.m2")
    i = ve.scalar_tensor_tensor(rows[0:1, 512:512 + w], psv[0:1, 0:w], 1.0,
                                rows[0:1, 0:w], ALU.mult, ALU.subtract)
    ctx.inc("ve", i, "ve", tag=f"{ph}.var")
    ctx.wait("sc", f"{ph}.var")
    i = sc.activation(rows[0:1, 0:w], rows[0:1, 512:512 + w],
                      AF.Ln, bias=H["epsc"][:])
    ctx.chain("sc", i)
    i = sc.activation(H["rows"][0:1, 1024:1024 + w], rows[0:1, 0:w],
                      AF.Exp, scale=-0.5)
    ctx.inc("sc", i, "sc", tag=f"{ph}.sd")
    ctx.wait("ve", f"{ph}.sd")
    i = ve.tensor_tensor(H["rows"][0:1, 512:512 + w], psm[0:1, 0:w],
                         H["rows"][0:1, 1024:1024 + w], ALU.mult)
    ctx.inc("ve", i, "ve", tag=f"{ph}.rows")
    ctx.wait("pe", f"{ph}.rows")
    ctx.wait("pe", f"{ph}.sd")
    pe.matmul(psrB[:, 0:w], H["onesc1f"][:], H["rows"][0:1, 1024:1024 + w],
              start=True, stop=True)
    i = pe.matmul(psmrB[:, 0:w], H["onesc1f"][:], H["rows"][0:1, 512:512 + w],
                  start=True, stop=True)
    ctx.inc("pe", i, "pe", tag=f"{ph}.bcast")
    ctx.wait("ve", f"{ph}.bcast")
    st = H["stF"]
    for c in range(NCHUNK):
        i = ve.tensor_tensor(st[0:, 0:w], x[:, c, o:o + w], psrB[:, 0:w],
                             ALU.mult)
        ctx.chain("ve", i)
        if out_bf is not None:
            i = ve.tensor_tensor(out_bf[:, c, out_off:out_off + w], st[0:, 0:w],
                                 psmrB[:, 0:w], ALU.subtract)
        if out_f32 is not None:
            i = ve.tensor_tensor(out_f32[c], st[0:, 0:w],
                                 psmrB[:, 0:w], ALU.subtract)
        ctx.chain("ve", i)
    ctx.mark(f"{ph}.done", "ve")


def phase_init(ctx, H):
    sy, gp, pe, sc, ve = (ctx.eng(e) for e in ENGS)
    i = sy.dma_start(out=H["x"][:],
                     in_=H["in"]["xT"].rearrange("(c p) n -> p c n", p=128))
    ctx.inc("sy", i, "dx", 16, tag="init.x")
    i = sy.dma_start(out=H["cg"][:], in_=H["in"]["cg"])
    ctx.inc("sy", i, "dx", 16)
    i = sy.dma_start(out=H["noncet"][:], in_=H["in"]["nonce"])
    ctx.inc("sy", i, "dx", 16)
    i = sy.dma_start(out=H["hxt"][:], in_=H["in"]["halfidx"])
    ctx.inc("sy", i, "dx", 16)
    ve.memset(H["statcol"][:], 1.0 / 1024.0)
    ve.memset(H["onesc1f"][:], 1.0)
    ve.memset(H["onesrB"][:], 1.0)
    ve.memset(H["onesc1B"][:], 1.0)
    ve.memset(H["onesrF8"][:], 1.0)
    ve.memset(H["neg8"][:], -8.0)
    i = ve.memset(H["epsc"][:], EPS)
    ctx.inc("ve", i, "ve", tag="init.consts")


def phase_l0ffn(ctx, H, which, lo, hi):
    sy, gp, pe, sc, ve = (ctx.eng(e) for e in ENGS)
    F = H["f0"]
    for nm, dst in (("Win", F["WinB"]), ("scat", F["scatB"])):
        i = gp.dma_start(out=dst[:], in_=H["in"][nm].rearrange(
            "(c p) n -> p c n", p=128))
        ctx.inc("gp", i, "dw", 16)
    for nm, dst in (("E", F["EB"]), ("Wout", F["WoutB"])):
        i = gp.dma_start(out=dst[:], in_=H["in"][nm].rearrange(
            "p (c n) -> p c n", n=128))
        ctx.inc("gp", i, "dw", 16)
    i = gp.dma_start(out=F["W1c"][:], in_=H["in"]["W1b"])
    ctx.inc("gp", i, "dw", 16)
    i = gp.dma_start(out=F["b1c"][:], in_=H["in"]["b1b"])
    ctx.inc("gp", i, "dw", 16)
    for nm, off, n in (("binr", 0, 64), ("b2r", 64, 64), ("boutr", 128, 1024)):
        i = gp.dma_start(out=F["rbf"][0:1, off:off + n], in_=H["in"][nm])
        ctx.inc("gp", i, "dw", 16)
    ctx.mark(f"{which}.w", "dw")
    for e in ("pe", "ve", "sc"):
        ctx.wait(e, f"{which}.w")
        ctx.wait(e, "init.consts")
        ctx.wait(e, "init.x")
    ps = H["ps"]
    for ti, (o, w) in enumerate(_tiles(lo, hi)):
        ph = f"{which}.t{ti}"
        pv = f"{which}.t{ti-1}"
        ln_tile(ctx, H, ph, o, w, out_bf=F["xnB"])
        H["_ln_cons"] = (("ve", f"{ph}.s"),)
        ctx.wait("pe", f"{ph}.done")
        if ti > 0:
            ctx.wait("pe", f"{pv}.gevac")
            ctx.wait("pe", f"{pv}.o{NCHUNK-1}")
        for c in range(NCHUNK):
            pe.matmul(ps[4][0:64, 0:w], F["WinB"][:, c], F["xnB"][:, c, 0:w],
                      start=(c == 0), stop=False)
        i = pe.matmul(ps[4][0:64, 0:w], F["rbf"][0:1, 0:64],
                      H["onesrB"][0:1, 0:w], start=False, stop=True)
        ctx.inc("pe", i, "pe", tag=f"{ph}.s")
        ctx.wait("ve", f"{ph}.s")
        if ti > 0:
            ctx.wait("ve", f"{pv}.hmm{NCHUNK-1}")
        i = ve.tensor_copy(F["sSB"][0:64, 0:w], ps[4][0:64, 0:w])
        ctx.inc("ve", i, "ve", tag=f"{ph}.sevac")
        for c8 in range(NCHUNK):
            bank = ps[5 - (c8 % 2)]
            ctx.wait("pe", f"{ph}.sevac")
            if c8 >= 2:
                ctx.wait("pe", f"{ph}.h{c8-2}")
            elif ti > 0:
                ctx.wait("pe", f"{pv}.o{6 + c8}")
            if c8 == 0 and ti > 0:
                ctx.wait("sc", f"{pv}.g")
            i = pe.matmul(bank[:, 0:w], F["EB"][0:64, c8], F["sSB"][0:64, 0:w],
                          start=True, stop=True)
            ctx.inc("pe", i, "pe", tag=f"{ph}.hmm{c8}")
            ctx.wait("sc", f"{ph}.hmm{c8}")
            i = sc.activation(F["hSB"][:, c8, 0:w], bank[:, 0:w], AF.Relu,
                              scale=F["W1c"][:, c8:c8 + 1],
                              bias=F["b1c"][:, c8:c8 + 1])
            ctx.inc("sc", i, "sc", tag=f"{ph}.h{c8}")
        for c8 in range(NCHUNK):
            ctx.wait("pe", f"{ph}.h{NCHUNK-1 if c8 == 0 else c8}")
            pe.matmul(ps[4][0:64, 0:w], F["scatB"][:, c8], F["hSB"][:, c8, 0:w],
                      start=(c8 == 0), stop=False)
        i = pe.matmul(ps[4][0:64, 0:w], F["rbf"][0:1, 64:128],
                      H["onesrB"][0:1, 0:w], start=False, stop=True)
        ctx.inc("pe", i, "pe", tag=f"{ph}.g")
        ctx.wait("ve", f"{ph}.g")
        if ti > 0:
            ctx.wait("ve", f"{pv}.omm{NCHUNK-1}")
        i = ve.tensor_copy(F["gSB"][0:64, 0:w], ps[4][0:64, 0:w])
        ctx.inc("ve", i, "ve", tag=f"{ph}.gevac")
        for oc in range(NCHUNK):
            bank = ps[5 - (oc % 2)]
            ctx.wait("pe", f"{ph}.gevac")
            if oc >= 2:
                ctx.wait("pe", f"{ph}.o{oc-2}")
            else:
                ctx.wait("pe", f"{ph}.h{6 + oc}")
            pe.matmul(bank[:, 0:w], F["WoutB"][0:64, oc], F["gSB"][0:64, 0:w],
                      start=True, stop=False)
            i = pe.matmul(bank[:, 0:w],
                          F["rbf"][0:1, 128 + 128 * oc:128 + 128 * (oc + 1)],
                          H["onesrB"][0:1, 0:w], start=False, stop=True)
            ctx.inc("pe", i, "pe", tag=f"{ph}.omm{oc}")
            ctx.wait("ve", f"{ph}.omm{oc}")
            i = ve.scalar_tensor_tensor(H["x"][:, oc, o:o + w], bank[:, 0:w],
                                        0.5, H["x"][:, oc, o:o + w],
                                        ALU.mult, ALU.add)
            ctx.inc("ve", i, "ve", tag=f"{ph}.o{oc}")


def phase_out(ctx, H):
    sy, gp, pe, sc, ve = (ctx.eng(e) for e in ENGS)
    ctx.wait_all("sy")
    i = sy.dma_start(out=H["out"]["y"].rearrange("(c p) n -> p c n", p=128),
                     in_=H["x"][:, :, HALO:TOK])
    ctx.inc("sy", i, "dx", 16, tag="out.y")
    ctx.wait("sy", "out.y")



def build_program(phases):
    import os
    nc = bass.Bass("TRN2", target_bir_lowering=False,
                   detect_race_conditions=bool(os.environ.get("KRACE")))
    IN = {}
    def inp(name, shape, dtype=F32):
        IN[name] = nc.dram_tensor(name, shape, dtype, kind="ExternalInput").ap()
    inp("xT", [1024, TOK])
    for nm in ("qT", "kT", "vW", "oT"):
        inp(nm, [1024, 1024])
    for nm, shp in (("qb", [1, 1024]), ("kb", [1, 1024]), ("vb", [1, 1024]),
                    ("ob", [1, 1024]), ("Win", [1024, 64]), ("binr", [1, 64]),
                    ("E", [64, 1024]), ("W1b", [128, 8]), ("b1b", [128, 8]),
                    ("scat", [1024, 64]), ("b2r", [1, 64]), ("Wout", [64, 1024]),
                    ("boutr", [1, 1024]), ("w1T", [1024, 4096]),
                    ("b1r", [1, 4096]), ("w2T", [4096, 1024]),
                    ("b2r1", [1, 1024]), ("W0z", [1024, 3072]),
                    ("W1z", [1024, 3072]), ("cg", [128, 1])):
        inp(nm, shp)
    for nm in ("maskA", "maskB", "maskC"):
        inp(nm, [128, 384], BF16)
    inp("dmask", [1024, HALF], BF16)
    inp("ident", [128, 128], BF16)
    inp("nonce", [1, 2], I32)
    IN["halfidx"] = nc.dram_tensor("halfidx", [1, 1], I32, kind="ExternalInput").ap()
    OUT = {"y": nc.dram_tensor("y", [1024, HALF], F32, kind="ExternalOutput").ap()}
    import os
    if os.environ.get("KDBG_ATT"):
        for nm, shp, dt_ in (("dbg_q", [128, 4, 512], BF16), ("dbg_k", [128, 4, 768], BF16),
                             ("dbg_vT", [128, 6, 512], BF16), ("dbg_at", [128, 384], BF16),
                             ("dbg_ex", [128, 384], BF16), ("dbg_oin", [128, 4, 128], BF16),
                             ("dbg_xn", [128, 8, 512], BF16)):
            OUT[nm] = nc.dram_tensor(nm, shp, dt_, kind="ExternalOutput").ap()
    xpage = nc.dram_tensor("xpage", [1024, HALF], F32)
    shm_c = nc.dram_tensor("shm_c", [2, 2, 1024], F32, addr_space="Shared")
    shm_f = nc.dram_tensor("shm_f", [2, 2, 1], I32, addr_space="Shared")

    with contextlib.ExitStack() as es:
        E = es.enter_context
        H = {"in": IN, "out": OUT, "xpage": xpage, "shm_c": shm_c, "shm_f": shm_f}
        H["x"] = E(nc.sbuf_tensor("xbuf", [128, NCHUNK, TOK], F32))
        H["xb"] = E(nc.sbuf_tensor("xbB", [128, NCHUNK, 512], BF16))
        H["sq"] = E(nc.sbuf_tensor("sqB", [128, NCHUNK, 512], BF16))
        H["stF"] = E(nc.sbuf_tensor("stF", [128, 512], F32))
        H["rows"] = E(nc.sbuf_tensor("rows", [1, 1536], F32))
        H["statcol"] = E(nc.sbuf_tensor("statcol", [128, 1], BF16))
        H["onesc1f"] = E(nc.sbuf_tensor("onesc1f", [1, 128], F32))
        H["onesrB"] = E(nc.sbuf_tensor("onesrB", [1, 512], BF16))
        H["onesc1B"] = E(nc.sbuf_tensor("onesc1B", [1, 128], BF16))
        H["onesrF8"] = E(nc.sbuf_tensor("onesrF8", [1, 512], F8E4))
        H["epsc"] = E(nc.sbuf_tensor("epsc", [1, 1], F32))
        H["neg8"] = E(nc.sbuf_tensor("neg8", [128, 1], F32))
        H["cg"] = E(nc.sbuf_tensor("sb_cg", [128, 1], F32))
        H["noncet"] = E(nc.sbuf_tensor("noncet", [1, 2], I32))
        H["hxt"] = E(nc.sbuf_tensor("hxt", [1, 1], I32))
        H["ps"] = [E(nc.psum_tensor(f"psb{i}", [128, 512], F32)) for i in range(8)]

        sems = {s: E(nc.semaphore(f"sem_{s}")) for s in SEMS}
        sems["pl"] = E(nc.semaphore("sem_pl"))
        block = E(nc.Block())

        ctx = Ctx(nc, sems)

        def run_phase(fn, barrier=True):
            cnt0 = dict(ctx.cnt)
            state0 = {k: H[k] for k in H if k.startswith("_")}
            def reset_state():
                for k in [k for k in H if k.startswith("_")]:
                    del H[k]
                H.update(state0)
            ctx.begin_pass(None, cnt0)
            fn()
            plan_final = dict(ctx.cnt)
            for name, reg in (("sy", block.sync), ("gp", block.gpsimd),
                              ("pe", block.tensor), ("sc", block.scalar),
                              ("ve", block.vector)):
                def body(e, name=name, cnt0=cnt0, fn=fn):
                    ctx.eng_handles[name] = e
                    ctx.begin_pass(name, cnt0)
                    reset_state()
                    if barrier:
                        ctx.wait_all(name)
                    ctx.emitted = {}
                    fn()
                    own = {"sy": ("dx",), "gp": ("dw", "da", "db"),
                           "pe": ("pe",), "sc": ("sc",), "ve": ("ve",)}[name]
                    for s_ in own:
                        want = plan_final[s_] - cnt0[s_]
                        got = ctx.emitted.get(s_, 0)
                        assert got == want, (
                            f"pass {name} sem {s_}: emitted {got} != planned {want}")
                    assert ctx.cnt == plan_final, (
                        f"pass {name}: counters diverged "
                        f"{ {k: (ctx.cnt[k], plan_final[k]) for k in ctx.cnt if ctx.cnt[k] != plan_final[k]} }")
                reg(body)
            ctx.live = None

        def scoped(names):
            s = contextlib.ExitStack()
            return s

        for p in phases:
            if p == "init":
                run_phase(lambda: phase_init(ctx, H), barrier=False)
            elif p in ("f0a", "f0b"):
                with contextlib.ExitStack() as sp:
                    Ep = sp.enter_context
                    H["f0"] = {
                        "WinB": Ep(nc.sbuf_tensor(f"{p}_WinB", [128, 8, 64], BF16)),
                        "EB": Ep(nc.sbuf_tensor(f"{p}_EB", [64, 8, 128], BF16)),
                        "scatB": Ep(nc.sbuf_tensor(f"{p}_scatB", [128, 8, 64], BF16)),
                        "WoutB": Ep(nc.sbuf_tensor(f"{p}_WoutB", [64, 8, 128], BF16)),
                        "W1c": Ep(nc.sbuf_tensor(f"{p}_W1c", [128, 8], F32)),
                        "b1c": Ep(nc.sbuf_tensor(f"{p}_b1c", [128, 8], F32)),
                        "rbf": Ep(nc.sbuf_tensor(f"{p}_rbf", [1, 1152], BF16)),
                        "sSB": Ep(nc.sbuf_tensor(f"{p}_sSB", [128, 512], BF16)),
                        "gSB": Ep(nc.sbuf_tensor(f"{p}_gSB", [64, 512], BF16)),
                        "hSB": Ep(nc.sbuf_tensor(f"{p}_hSB", [128, 8, 512], BF16)),
                        "xnB": Ep(nc.sbuf_tensor(f"{p}_xnB", [128, 8, 512], BF16)),
                    }
                    lo = 0 if p == "f0a" else HALO
                    run_phase(lambda p=p, lo=lo: phase_l0ffn(ctx, H, p, lo, TOK))
                    del H["f0"]
            elif p == "att":
                with contextlib.ExitStack() as sp:
                    Ep = sp.enter_context
                    H["att"] = {
                        "qTh": Ep(nc.sbuf_tensor("at_qTh", [128, 8, 512], BF16)),
                        "kTh": Ep(nc.sbuf_tensor("at_kTh", [128, 8, 512], BF16)),
                        "vWh": Ep(nc.sbuf_tensor("at_vWh", [128, 8, 512], BF16)),
                        "oTh": Ep(nc.sbuf_tensor("at_oTh", [128, 4, 1024], BF16)),
                        "qbuf": Ep(nc.sbuf_tensor("at_qbuf", [128, 4, 512], BF16)),
                        "kbuf": Ep(nc.sbuf_tensor("at_kbuf", [128, 4, 768], BF16)),
                        "vTbuf": Ep(nc.sbuf_tensor("at_vT", [128, 6, 8, 66], BF16)),
                        "o_in": Ep(nc.sbuf_tensor("at_oin", [128, 2, 4, 128], BF16)),
                        "rowsA": Ep(nc.sbuf_tensor("at_rows", [1, 2560], BF16)),
                        "expA": Ep(nc.sbuf_tensor("at_expA", [128, 8, 384], BF16)),
                        "bcS": Ep(nc.sbuf_tensor("at_bcS", [64, 2, 128], F32)),
                        "oraw": Ep(nc.sbuf_tensor("at_oraw", [65, 4, 128], F32)),
                        "rrow": Ep(nc.sbuf_tensor("at_rrow", [1, 2, 128], BF16)),
                        "lnw": Ep(nc.sbuf_tensor("at_lnw", [1, 2, 128], F32)),
                        "maskA": Ep(nc.sbuf_tensor("at_maskA", [128, 384], BF16)),
                        "maskB": Ep(nc.sbuf_tensor("at_maskB", [128, 384], BF16)),
                        "maskC": Ep(nc.sbuf_tensor("at_maskC", [128, 384], BF16)),
                        "xnf": Ep(nc.sbuf_tensor("at_xnf", [128, 8, TOK], BF16)),
                    }
                    run_phase(lambda: phase_attn(ctx, H))
                    del H["att"]
            elif p in ("f1a", "f1b"):
                with contextlib.ExitStack() as sp:
                    Ep = sp.enter_context
                    H["f1"] = {
                        "w1q": Ep(nc.sbuf_tensor(f"{p}_w1q", [128, 8, 4096], F8E4)),
                        "w2q": Ep(nc.sbuf_tensor(f"{p}_w2q", [128, 32, 1024], F8E4)),
                        "xq": Ep(nc.sbuf_tensor(f"{p}_xq", [128, 8, 512], F8E4)),
                        "hq": Ep(nc.sbuf_tensor(f"{p}_hq", [128, 32, 512], F8E4)),
                        "y16": Ep(nc.sbuf_tensor(f"{p}_y16", [128, 512], FP16)),
                        "h16": Ep(nc.sbuf_tensor(f"{p}_h16", [128, 512], FP16)),
                        "dm": Ep(nc.sbuf_tensor(f"{p}_dm", [128, 8, 512], BF16)),
                        "rf8": Ep(nc.sbuf_tensor(f"{p}_rf8", [1, 5120], F8E4)),
                    }
                    run_phase(lambda p=p: phase_l1ffn(ctx, H, p))
                    del H["f1"]
            elif p == "scan":
                with contextlib.ExitStack() as sp:
                    Ep = sp.enter_context
                    H["sn"] = {
                        "Wz": [Ep(nc.sbuf_tensor(f"sn_Wz{j}", [128, 8, 384], BF16))
                               for j in range(2)],
                        "gA": Ep(nc.sbuf_tensor("sn_gA", [128, 8, 2048], BF16)),
                        "f": Ep(nc.sbuf_tensor("sn_f", [128, 8, 512], F32)),
                        "msk": Ep(nc.sbuf_tensor("sn_msk", [128, 512], U8)),
                        "carH": Ep(nc.sbuf_tensor("sn_carH", [128, 8], F32)),
                        "carA": Ep(nc.sbuf_tensor("sn_carA", [128, 8], F32)),
                        "Hin": Ep(nc.sbuf_tensor("sn_Hin", [128, 8], F32)),
                        "flag": Ep(nc.sbuf_tensor("sn_flag", [1, 1], I32)),
                        "hnF": Ep(nc.sbuf_tensor("sn_hnF", [128, 8, 2048], BF16)),
                        "xst": Ep(nc.sbuf_tensor("sn_xst", [128, 2048], F32)),
                        "zeros512": Ep(nc.sbuf_tensor("zeros512", [128, 512], F32)),
                    }
                    run_phase(lambda: phase_scan(ctx, H))
                    del H["sn"]
            elif p == "out":
                run_phase(lambda: phase_out(ctx, H))
            else:
                raise ValueError(p)
    return nc



_BUILD_CACHE = {}

def phase_scan(ctx, H):
    sy, gp, pe, sc, ve = (ctx.eng(e) for e in ENGS)
    ps = H["ps"]
    S = H["sn"]
    x = H["x"]
    xpage = H["xpage"]

    for c in range(NCHUNK):
        i = sy.dma_start(out=xpage[128 * c:128 * (c + 1), :], in_=x[:, c, HALO:TOK])
        ctx.inc("sy", i, "dx", 16)
    ctx.mark("scan.pgout", "dx")
    for e in ("pe", "ve", "sc"):
        ctx.wait(e, "init.consts")
    i = ve.memset(S["zeros512"][:], 0.0)
    ctx.inc("ve", i, "ve", tag="scan.z0")
    H["_ln_cons"] = (("ve", "scan.pgout"),)
    for ti, (o, w) in enumerate(_tiles(HALO, TOK)):
        ln_tile(ctx, H, f"snln.t{ti}", o, w,
                out_f32=[x[:, c, o:o + w] for c in range(NCHUNK)])
        H["_ln_cons"] = ()

    seen_par = set()
    seen_ff = [False]
    for s in (0, 1):
        sp = f"sn{s}"
        W = H["in"]["W0z"] if s == 0 else H["in"]["W1z"]
        if s == 0:
            ctx.wait("sc", "snln.t3.done")
        else:
            ctx.wait("sc", "sn0.corr")
        for c in range(NCHUNK):
            for tt in range(4):
                i = sc.copy(S["hnF"][:, c, 512 * tt:512 * (tt + 1)],
                            x[:, c, HALO + 512 * tt:HALO + 512 * (tt + 1)])
        ctx.inc("sc", i, "sc", tag=f"{sp}.hn")
        for c in range(NCHUNK):
            cp = f"{sp}.c{c}"
            j = c % 2
            if c >= 2:
                ctx.wait("gp", f"{sp}.c{c-2}.zmmlast")
            elif s == 1:
                ctx.wait("gp", f"sn0.c{6 + c}.zmmlast")
            for k3, off in ((0, 0), (1, 1024), (2, 2048)):
                i = gp.dma_start(out=S["Wz"][j][:, :, 128 * k3:128 * (k3 + 1)],
                                 in_=W[:, off + 128 * c:off + 128 * (c + 1)]
                                 .rearrange("(cc p) n -> p cc n", p=128))
                ctx.inc("gp", i, "da" if j == 0 else "db", 16)
            ctx.mark(f"{cp}.w", "da" if j == 0 else "db")
            ctx.wait("pe", f"{cp}.w")
            ctx.wait("pe", f"{sp}.hn")
            for t in range(4):
                tp = f"{cp}.t{t}"
                o = HALO + 512 * t
                par = t % 2
                zb = (4, 5, 3) if par == 0 else (0, 1, 2)
                if par in seen_par:
                    ctx.wait("pe", f"zfree{par}_ve")
                    ctx.wait("pe", f"zfree{par}_sc")
                for k3 in range(3):
                    for cc in range(NCHUNK):
                        i = pe.matmul(ps[zb[k3]][:, 0:512],
                                      S["Wz"][j][:, cc, 128 * k3:128 * (k3 + 1)],
                                      S["hnF"][:, cc, 512 * t:512 * (t + 1)],
                                      start=(cc == 0), stop=(cc == NCHUNK - 1))
                ctx.inc("pe", i, "pe", tag=f"{tp}.zmm")
                if t == 3:
                    ctx.marks[f"{cp}.zmmlast"] = ctx.marks[f"{tp}.zmm"]
                f = S["f"]
                ctx.wait("sc", f"{tp}.zmm")
                if seen_ff[0]:
                    ctx.wait("sc", "snffree")
                i = sc.activation(f[:, 0, 0:512], ps[zb[1]][:, 0:512], AF.Sigmoid,
                                  scale=-1.0)
                ctx.chain("sc", i)
                i = sc.activation(f[:, 1, 0:512], ps[zb[1]][:, 0:512], AF.Sigmoid)
                ctx.chain("sc", i)
                i = sc.activation(f[:, 2, 0:512], ps[zb[0]][:, 0:512], AF.Sigmoid)
                ctx.inc("sc", i, "sc", tag=f"{tp}.sig")
                ctx.wait("ve", f"{tp}.zmm")
                ctx.wait("ve", f"{tp}.sig")
                i = ve.scalar_tensor_tensor(f[:, 3, 0:512], ps[zb[0]][:, 0:512],
                                            0.5, f[:, 2, 0:512],
                                            ALU.add, ALU.subtract)
                ctx.inc("ve", i, "ve", tag=f"{tp}.gt")
                ctx.mark(f"{tp}.zdone_hg", "ve")
                ctx.wait("sc", f"{tp}.gt")
                i = sc.activation(f[:, 5, 0:512], f[:, 3, 0:512], AF.Relu)
                ctx.chain("sc", i)
                i = sc.activation(f[:, 4, 0:512], ps[zb[2]][:, 0:512], AF.Sigmoid)
                ctx.inc("sc", i, "sc", tag=f"{tp}.acts")
                ctx.wait("ve", f"{tp}.acts")
                i = ve.tensor_tensor(f[:, 5, 0:512], f[:, 2, 0:512],
                                     f[:, 5, 0:512], ALU.add)
                ctx.chain("ve", i)
                i = ve.tensor_tensor(f[:, 6, 0:512], f[:, 1, 0:512],
                                     f[:, 5, 0:512], ALU.mult)
                ctx.chain("ve", i)
                i = ve.tensor_tensor_scan(
                    f[:, 7, 0:512], f[:, 0, 0:512], f[:, 6, 0:512],
                    0.0 if t == 0 else S["carH"][:, c:c + 1], ALU.mult, ALU.add)
                ctx.chain("ve", i)
                i = ve.tensor_copy(S["carH"][:, c:c + 1], f[:, 7, 511:512])
                ctx.chain("ve", i)
                i = ve.tensor_tensor_scan(
                    f[:, 2, 0:512], f[:, 0, 0:512], S["zeros512"][:, 0:512],
                    1.0 if t == 0 else S["carA"][:, c:c + 1], ALU.mult, ALU.add)
                ctx.chain("ve", i)
                i = ve.tensor_copy(S["carA"][:, c:c + 1], f[:, 2, 511:512])
                ctx.chain("ve", i)
                i = ve.tensor_tensor(S["gA"][:, c, 512 * t:512 * (t + 1)],
                                     f[:, 4, 0:512], f[:, 2, 0:512], ALU.mult)
                ctx.chain("ve", i)
                hsl = x[:, c, o:o + 512]
                i = ve.tensor_tensor(f[:, 6, 0:512], f[:, 7, 0:512], hsl,
                                     ALU.subtract)
                ctx.chain("ve", i)
                i = ve.tensor_tensor(f[:, 3, 0:512], f[:, 4, 0:512],
                                     f[:, 6, 0:512], ALU.mult)
                ctx.chain("ve", i)
                i = ve.tensor_tensor(hsl, hsl, f[:, 3, 0:512], ALU.add)
                ctx.inc("ve", i, "ve", tag=f"{tp}.hupd")
                ctx.mark("snffree", "ve")
                seen_ff[0] = True
                ctx.marks[f"zfree{par}_ve"] = ctx.marks[f"{tp}.zdone_hg"]
                ctx.marks[f"zfree{par}_sc"] = ctx.marks[f"{tp}.acts"]
                seen_par.add(par)
        ctx.mark(f"{sp}.localdone", "ve")
        shm_c, shm_f = H["shm_c"], H["shm_f"]
        ctx.wait("gp", f"{sp}.localdone")
        if ctx.live == "gp":
            e = ctx.eng_handles["gp"]
            with (e.register(f"myh{s}") as myh, e.register(f"oth{s}") as oth,
                  e.register(f"off{s}") as off, e.register(f"want{s}") as want,
                  e.register(f"fv{s}") as fv, e.register(f"diff{s}") as diff,
                  e.register(f"cnt{s}") as cnt):
                e.reg_load(myh, H["hxt"][0:1, 0:1])
                e.reg_mov(oth, 1)
                e.reg_sub(oth, oth, myh)
                e.reg_mov(off, 2048)
                e.reg_mul(off, off, myh)
                e.reg_add(off, off, 1024 * s)
                e.dma_start(out=bass.AP(shm_c, off, [[8, 128], [1, 8]]),
                            in_=S["carH"][:]).then_inc(ctx.sems["dw"], 16)
                e.wait_ge(ctx.sems["dw"], ctx.cnt["dw"] + 16)
                e.reg_mov(off, 2)
                e.reg_mul(off, off, myh)
                e.reg_add(off, off, s)
                e.dma_start(out=bass.AP(shm_f, off, [[1, 1], [1, 1]]),
                            in_=H["noncet"][0:1, s:s + 1]).then_inc(
                                ctx.sems["dw"], 16)
                e.reg_load(want, H["noncet"][0:1, s:s + 1])
                e.reg_mov(off, 2)
                e.reg_mul(off, off, oth)
                e.reg_add(off, off, s)
                e.reg_mov(cnt, 0)
                e.dma_start(out=S["flag"][:],
                            in_=bass.AP(shm_f, off, [[1, 1], [1, 1]])).then_inc(
                                ctx.sems["pl"], 16)
                e.reg_add(cnt, cnt, 16)
                e.wait_ge(ctx.sems["pl"], cnt)
                e.reg_load(fv, S["flag"][0:1, 0:1])
                e.reg_sub(diff, fv, want)
                with e.While(diff):
                    e.dma_start(out=S["flag"][:],
                                in_=bass.AP(shm_f, off, [[1, 1], [1, 1]])).then_inc(
                                    ctx.sems["pl"], 16)
                    e.reg_add(cnt, cnt, 16)
                    e.wait_ge(ctx.sems["pl"], cnt)
                    e.reg_load(fv, S["flag"][0:1, 0:1])
                    e.reg_sub(diff, fv, want)
                e.reg_mov(off, 2048)
                e.reg_mul(off, off, oth)
                e.reg_add(off, off, 1024 * s)
                e.dma_start(out=S["Hin"][:],
                            in_=bass.AP(shm_c, off, [[8, 128], [1, 8]])).then_inc(
                                ctx.sems["dw"], 16)
        ctx.cnt["dw"] += 48
        if ctx.live == "gp":
            ctx.emitted["dw"] = ctx.emitted.get("dw", 0) + 48
        ctx.mark(f"{sp}.hin", "dw")
        ctx.wait("ve", f"{sp}.hin")
        ctx.wait("ve", f"{sp}.localdone")
        i = ve.tensor_scalar_mul(S["Hin"][:], S["Hin"][:], H["cg"][:, 0:1])
        ctx.chain("ve", i)
        for c in range(NCHUNK):
            for t in range(4):
                i = ve.scalar_tensor_tensor(
                    x[:, c, HALO + 512 * t:HALO + 512 * (t + 1)],
                    S["gA"][:, c, 512 * t:512 * (t + 1)], S["Hin"][:, c:c + 1],
                    x[:, c, HALO + 512 * t:HALO + 512 * (t + 1)],
                    ALU.mult, ALU.add)
        ctx.inc("ve", i, "ve", tag=f"{sp}.corr")
    ctx.wait("sy", "sn1.corr")
    ctx.wait("ve", "sn1.corr")
    for c in range(NCHUNK):
        if c >= 1:
            ctx.wait("sy", f"scan.xr{c-1}")
        i = sy.dma_start(out=S["xst"][:], in_=xpage[128 * c:128 * (c + 1), :])
        ctx.inc("sy", i, "dx", 16, tag=f"scan.xld{c}")
        ctx.wait("ve", f"scan.xld{c}")
        i = ve.tensor_tensor(x[:, c, HALO:TOK], x[:, c, HALO:TOK],
                             S["xst"][:], ALU.add)
        ctx.inc("ve", i, "ve", tag=f"scan.xr{c}")
    H["_ln_cons"] = (("ve", f"scan.xr{NCHUNK-1}"),)



_BUILD_CACHE = {}


def _get_program(phases):
    key = tuple(phases)
    if key not in _BUILD_CACHE:
        _BUILD_CACHE[key] = build_program(list(phases))
    return _BUILD_CACHE[key]


def _prep_static(inputs):
    f32 = np.float32
    asc = np.ascontiguousarray
    g = {}
    g["qT"] = asc(inputs["L0_attn_q_w"].T.astype(f32))
    g["kT"] = asc(inputs["L0_attn_k_w"].T.astype(f32))
    g["vW"] = asc(inputs["L0_attn_v_w"].T.astype(f32))
    g["oT"] = asc(inputs["L0_attn_o_w"].T.astype(f32))
    for nm, src in (("qb", "L0_attn_q_b"), ("kb", "L0_attn_k_b"),
                    ("vb", "L0_attn_v_b"), ("ob", "L0_attn_o_b")):
        g[nm] = asc(inputs[src].astype(f32).reshape(1, -1))
    g["Win"] = asc(inputs["L0_ffn_Win"].astype(f32))
    g["binr"] = asc(inputs["L0_ffn_bin_"].astype(f32).reshape(1, -1))
    Emat = np.zeros((64, 1024), f32)
    for m in range(64):
        Emat[m, m * 16:(m + 1) * 16] = 1.0
    g["E"] = Emat
    g["W1b"] = asc(inputs["L0_ffn_W1"].astype(f32).reshape(-1).reshape(8, 128).T)
    g["b1b"] = asc(inputs["L0_ffn_b1"].astype(f32).reshape(-1).reshape(8, 128).T)
    scat = np.zeros((1024, 64), f32)
    W2 = inputs["L0_ffn_W2"].astype(f32)
    for m in range(64):
        scat[m * 16:(m + 1) * 16, m] = W2[m]
    g["scat"] = scat
    g["b2r"] = asc(inputs["L0_ffn_b2"].astype(f32).reshape(1, -1))
    g["Wout"] = asc(inputs["L0_ffn_Wout"].astype(f32))
    g["boutr"] = asc(inputs["L0_ffn_bout"].astype(f32).reshape(1, -1))
    g["w1T"] = asc(inputs["L1_ffn_w1"].astype(f32).T)
    g["b1r"] = asc(inputs["L1_ffn_b1"].astype(f32).reshape(1, -1))
    g["w2T"] = asc(inputs["L1_ffn_w2"].astype(f32).T)
    g["b2r1"] = asc(inputs["L1_ffn_b2"].astype(f32).reshape(1, -1))
    g["W0z"] = asc(inputs["L1_attn_W0"].astype(f32))
    g["W1z"] = asc(inputs["L1_attn_W1"].astype(f32))
    g["ident"] = np.eye(128, dtype=ml_dtypes.bfloat16)

    r = np.arange(128)[:, None]
    c = np.arange(384)[None, :]
    band = ((c - r >= 1) & (c - r <= 256)).astype(f32)
    def mk(extra_lo):
        m = band * (c >= extra_lo)
        m = np.concatenate([m[:, 128 * b:128 * (b + 1)].T for b in range(3)],
                           axis=1)
        return np.ascontiguousarray(m).astype(ml_dtypes.bfloat16)
    g["maskC"] = mk(-1)
    g["_maskA_start"] = mk(256)
    g["_maskB_start"] = mk(128)

    import jax
    with jax.default_device(jax.devices("cpu")[0]):
        keep = np.asarray(jax.random.bernoulli(
            jax.random.key(42, impl="threefry2x32"), 0.8, (B, T, D)))
    g["_keep"] = keep
    return g


def _core_inputs(g, inputs, core, nonce):
    b, h = core // 2, core % 2
    x = inputs["L0_attn_x"].astype(np.float32)
    lo = h * HALF - HALO
    if lo < 0:
        shard = np.concatenate(
            [np.zeros((HALO, D), np.float32), x[b, 0:HALF]], axis=0)
    else:
        shard = x[b, lo:lo + TOK]
    d = {k: v for k, v in g.items() if not k.startswith("_")}
    d["xT"] = np.ascontiguousarray(shard.T)
    if h == 0:
        d["maskA"], d["maskB"] = g["_maskA_start"], g["_maskB_start"]
        d["cg"] = np.zeros((128, 1), np.float32)
    else:
        d["maskA"], d["maskB"] = g["maskC"], g["maskC"]
        d["cg"] = np.ones((128, 1), np.float32)
    keep = g["_keep"][b, h * HALF:(h + 1) * HALF]
    dm = (keep.astype(np.float32) * 0.625).T
    d["dmask"] = np.ascontiguousarray(dm.astype(ml_dtypes.bfloat16))
    d["nonce"] = np.array([[nonce, nonce + 1]], np.int32)
    d["halfidx"] = np.array([[h]], np.int32)
    return d


FUSED_PHASES = ("init", "f0a", "att", "f0b", "f1a", "scan", "f1b", "out")

PHASE_GROUPS = [FUSED_PHASES]


def kernel(**inputs):
    g = _prep_static(inputs)
    x = inputs["L0_attn_x"].astype(np.float32)
    out = np.empty((B, T, D), np.float32)
    cur = x
    for gi, group in enumerate(PHASE_GROUPS):
        phases = tuple(group)
        nc = _get_program(phases)
        nonce = int(np.random.randint(1, 2**30)) + gi
        ins_full = dict(inputs)
        ins_full["L0_attn_x"] = cur
        maps = [_core_inputs(g, ins_full, c, nonce) for c in range(8)]
        res = run_bass_kernel_spmd(nc, maps, core_ids=list(range(8)))
        nxt = np.empty_like(cur)
        for c in range(8):
            b, h = c // 2, c % 2
            nxt[b, h * HALF:(h + 1) * HALF] = res.results[c]["y"].T
        cur = nxt
    out[:] = cur
    return out


def phase_attn(ctx, H):
    sy, gp, pe, sc, ve = (ctx.eng(e) for e in ENGS)
    ps = H["ps"]
    A = H["att"]
    tiles = _tiles(0, TOK)

    for nm in ("maskA", "maskB", "maskC"):
        i = sy.dma_start(out=A[nm][:], in_=H["in"][nm])
        ctx.inc("sy", i, "dx", 16, tag="att.cst" if nm == "maskC" else None)
    i = ve.memset(A["vTbuf"][:, :, :, 64:65], 1.0)
    ctx.inc("ve", i, "ve")
    for e in ("pe", "ve", "sc"):
        ctx.wait(e, "init.consts")
        ctx.wait(e, "init.x")
    for ti, (o, w) in enumerate(_tiles(0, TOK)):
        ln_tile(ctx, H, f"attln.t{ti}", o, w, out_bf=A["xnf"], out_off=o)
    for half in (0, 1):
        hp = f"att{half}"
        ctx.wait("pe", "att.cst")
        if half == 1:
            ctx.wait("gp", "att0.lastq")
            ctx.wait("gp", "att0.lastprod")
        for nm, dst in (("qT", A["qTh"]), ("kT", A["kTh"]), ("vW", A["vWh"])):
            i = gp.dma_start(out=dst[:], in_=H["in"][nm][:, 512 * half:512 * (half + 1)]
                             .rearrange("(c p) n -> p c n", p=128))
            ctx.inc("gp", i, "dw", 16)
        i = gp.dma_start(out=A["oTh"][:],
                         in_=H["in"]["oT"][512 * half:512 * (half + 1), :]
                         .rearrange("(c p) n -> p c n", p=128))
        ctx.inc("gp", i, "dw", 16)
        for nm, off in (("qb", 0), ("kb", 512), ("vb", 1024)):
            i = gp.dma_start(out=A["rowsA"][0:1, off:off + 512],
                             in_=H["in"][nm][0:1, 512 * half:512 * half + 512])
            ctx.inc("gp", i, "dw", 16)
        i = gp.dma_start(out=A["rowsA"][0:1, 1536:2560], in_=H["in"]["ob"])
        ctx.inc("gp", i, "dw", 16, tag=f"{hp}.w")
        for e in ("pe", "ve", "sc"):
            ctx.wait(e, f"{hp}.w")

        prev_qp = "att0.t4.j15" if half == 1 else None
        prev2_qp = "att0.t4.j14" if half == 1 else None
        pending = [None]
        prev_op = ["att0.t4.j15" if half == 1 else None]
        for e in ("pe", "ve", "sc"):
            ctx.wait(e, "attln.t4.done")
        for ti, (o, w) in enumerate(tiles):
            ph = f"{hp}.t{ti}"
            lastj = None if prev_qp is None else prev_qp
            nkb = w // 128
            kb0 = o // 128
            if pending[0] is not None:
                emit_oproj(*pending[0])
                pending[0] = None
            for oc4 in range(4):
                bank = ps[4 if oc4 % 2 == 0 else 2]
                if oc4 == 0 and lastj is not None:
                    ctx.wait("pe", f"{lastj}.exp5")
                if oc4 == 1:
                    ctx.wait("pe", f"{lastj}.og1") if lastj is not None else None
                if oc4 >= 2:
                    ctx.wait("pe", f"{ph}.q{oc4-2}.ev")
                for c in range(NCHUNK):
                    pe.matmul(bank[:, 0:w], A["qTh"][:, c, 128 * oc4:128 * (oc4 + 1)],
                              A["xnf"][:, c, o:o + w], start=(c == 0), stop=False)
                i = pe.matmul(bank[:, 0:w], A["rowsA"][0:1, 128 * oc4:128 * (oc4 + 1)],
                              H["onesrB"][0:1, 0:w], start=False, stop=True)
                ctx.inc("pe", i, "pe", tag=f"{ph}.q{oc4}.mm")
                ctx.wait("sc", f"{ph}.q{oc4}.mm")
                if oc4 == 0 and lastj is not None:
                    ctx.wait("sc", f"{lastj}.smm7")
                i = sc.mul(A["qbuf"][:, oc4, 0:w], bank[:, 0:w], 0.125)
                ctx.inc("sc", i, "sc", tag=f"{ph}.q{oc4}.ev")
            for oc4 in range(4):
                bank = ps[4 if oc4 % 2 == 0 else 2]
                ctx.wait("pe", f"{ph}.q{oc4+2}.ev" if oc4 <= 1
                         else f"{ph}.k{oc4-2}.ev")
                for c in range(NCHUNK):
                    pe.matmul(bank[:, 0:w], A["kTh"][:, c, 128 * oc4:128 * (oc4 + 1)],
                              A["xnf"][:, c, o:o + w], start=(c == 0), stop=False)
                i = pe.matmul(bank[:, 0:w],
                              A["rowsA"][0:1, 512 + 128 * oc4:512 + 128 * (oc4 + 1)],
                              H["onesrB"][0:1, 0:w], start=False, stop=True)
                ctx.inc("pe", i, "pe", tag=f"{ph}.k{oc4}.mm")
                ctx.wait("sc", f"{ph}.k{oc4}.mm")
                for j in range(nkb):
                    sl = (kb0 + j) % 6
                    i = sc.copy(A["kbuf"][:, oc4, 128 * sl:128 * (sl + 1)],
                                bank[:, 128 * j:128 * (j + 1)])
                ctx.inc("sc", i, "sc", tag=f"{ph}.k{oc4}.ev")
            for j in range(nkb):
                bank = ps[4 if j % 2 == 0 else 2]
                ctx.wait("pe", f"{ph}.k{j+2}.ev" if j <= 1
                         else f"{ph}.v{j-2}.ev")
                for c in range(NCHUNK):
                    pe.matmul(bank[:, 0:512], A["xnf"][:, c, o + 128 * j:o + 128 * (j + 1)],
                              A["vWh"][:, c, 0:512], start=(c == 0), stop=False)
                i = pe.matmul(bank[:, 0:512], H["onesc1B"][0:1, 0:128],
                              A["rowsA"][0:1, 1024:1536], start=False, stop=True)
                ctx.inc("pe", i, "pe", tag=f"{ph}.v{j}.mm")
                ctx.wait("sc", f"{ph}.v{j}.mm")
                if j == 0 and lastj is not None:
                    ctx.wait("sc", f"{lastj}.av7")
                sl = (kb0 + j) % 6
                i = sc.copy(A["vTbuf"][:, sl, :, 0:64],
                            bank[:, 0:512].rearrange("p (h n) -> p h n", h=8))
                ctx.inc("sc", i, "sc", tag=f"{ph}.v{j}.ev")
            ctx.mark(f"{ph}.prodmm", "ve")
            def emit_oproj(qp_, qj_, qlo_, qt_):
                bank = ps[2]
                for grp in range(2):
                    ctx.wait("pe", f"{qp_}.ot7")
                    if grp == 1:
                        ctx.wait("pe", f"{qp_}.og0")
                    elif prev_op[0] is not None:
                        ctx.wait("pe", f"{prev_op[0]}.og1")
                        prev_op[0] = None
                    for oc4i in range(4):
                        oc = 4 * grp + oc4i
                        for c4 in range(4):
                            i = pe.matmul(bank[:, 128 * oc4i:128 * (oc4i + 1)],
                                          A["oTh"][:, c4, 128 * oc:128 * (oc + 1)],
                                          A["o_in"][:, qj_ % 2, c4, 0:128],
                                          start=(c4 == 0),
                                          stop=(c4 == 3 and half == 1))
                        if half == 0:
                            i = pe.matmul(bank[:, 128 * oc4i:128 * (oc4i + 1)],
                                          A["rowsA"][0:1, 1536 + 128 * oc:1536 + 128 * (oc + 1)],
                                          H["onesrB"][0:1, 0:128],
                                          start=False, stop=True)
                    ctx.inc("pe", i, "pe", tag=f"{qp_}.omm{grp}")
                    ctx.wait("ve", f"{qp_}.omm{grp}")
                    xs = H["x"][:, 4 * grp:4 * grp + 4,
                                qlo_ + 128 * qt_:qlo_ + 128 * (qt_ + 1)]
                    i = ve.scalar_tensor_tensor(
                        xs, bank[:, 0:512].rearrange("p (c n) -> p c n", c=4),
                        1.0, xs, ALU.mult, ALU.add)
                    ctx.inc("ve", i, "ve", tag=f"{qp_}.og{grp}")
                prev_op[0] = qp_
            qlo = max(o, HALO)
            nq = (o + w - qlo) // 128
            for qt in range(nq):
                qj = (qlo - HALO) // 128 + qt
                qoff = qlo - o + 128 * qt
                qp = f"{ph}.j{qj}"
                mask = (A["maskA"] if qj == 0 else
                        A["maskB"] if qj == 1 else A["maskC"])
                pq = prev_qp
                ppq = prev2_qp
                SBK = (0, 1, 4)
                def emit_smm(hh):
                    sb = ps[SBK[hh % 3]]
                    lo64 = 64 * (hh % 2)
                    ch = hh // 2
                    if hh >= 3:
                        ctx.wait("pe", f"{qp}.exp{hh-3}")
                    else:
                        if hh == 2 and qt == 0:
                            ctx.wait("pe", f"{ph}.v{nkb-2}.ev")
                        elif pq is not None:
                            ctx.wait("pe", f"{pq}.exp{ {0: 6, 1: 7, 2: 5}[hh] }")
                    qsl = A["qbuf"][lo64:lo64 + 64, ch, qoff:qoff + 128]
                    for b3 in range(3):
                        sl = (qj + b3) % 6
                        i = pe.matmul(sb[:, 128 * b3:128 * (b3 + 1)],
                                      A["kbuf"][lo64:lo64 + 64, ch, 128 * sl:128 * (sl + 1)],
                                      qsl, start=True, stop=True)
                    ctx.inc("pe", i, "pe", tag=f"{qp}.smm{hh}")
                def emit_exp(hh):
                    sb = ps[SBK[hh % 3]]
                    ctx.wait("sc", f"{qp}.smm{hh}")
                    i = sc.activation(A["expA"][:, hh, 0:384], sb[:, 0:384],
                                      AF.Exp, bias=H["neg8"][:, 0:1])
                    ctx.inc("sc", i, "sc", tag=f"{qp}.exp{hh}")
                def emit_mt(hh):
                    ctx.wait("ve", f"{qp}.exp{hh}")
                    i = ve.tensor_tensor(A["expA"][:, hh, 0:384],
                                         A["expA"][:, hh, 0:384],
                                         mask[:], ALU.mult)
                    ctx.inc("ve", i, "ve", tag=f"{qp}.mt{hh}")
                def emit_av(hh):
                    ab = ps[3 if hh % 2 == 0 else 5]
                    ctx.wait("pe", f"{qp}.mt{hh}")
                    if hh >= 2:
                        ctx.wait("pe", f"{qp}.orv{hh-2}")
                    elif pq is not None:
                        ctx.wait("pe", f"{pq}.orv{6 + hh}")
                    for b3 in range(3):
                        sl = (qj + b3) % 6
                        i = pe.matmul(ab[0:65, 0:128],
                                      A["vTbuf"][:, sl, hh, 0:65],
                                      A["expA"][:, hh, 128 * b3:128 * (b3 + 1)],
                                      start=(b3 == 0), stop=(b3 == 2))
                    ctx.inc("pe", i, "pe", tag=f"{qp}.av{hh}")
                def emit_orv(hh):
                    ab = ps[3 if hh % 2 == 0 else 5]
                    ctx.wait("sc", f"{qp}.av{hh}")
                    if hh >= 4:
                        ctx.wait("sc", f"{qp}.ot{hh-4}")
                    elif pq is not None:
                        ctx.wait("sc", f"{pq}.ot{4 + hh}")
                    i = sc.copy(A["oraw"][0:65, hh % 4, 0:128], ab[0:65, 0:128])
                    ctx.inc("sc", i, "sc", tag=f"{qp}.orv{hh}")
                def emit_rc(hh):
                    ctx.wait("sc", f"{qp}.orv{hh}")
                    i = sc.activation(A["lnw"][0:1, hh % 2, 0:128],
                                      A["oraw"][64:65, hh % 4, 0:128], AF.Ln)
                    ctx.chain("sc", i)
                    if hh >= 2:
                        ctx.wait("sc", f"{qp}.bc{hh-2}")
                    elif pq is not None:
                        ctx.wait("sc", f"{pq}.bc{6 + hh}")
                    with ctx.nc.allow_low_precision(
                            reason="softmax 1/sum broadcast via bf16 matmul"):
                        i = sc.activation(A["rrow"][0:1, hh % 2, 0:128],
                                          A["lnw"][0:1, hh % 2, 0:128],
                                          AF.Exp, scale=-1.0)
                    ctx.inc("sc", i, "sc", tag=f"{qp}.rc{hh}")
                def emit_bc(hh):
                    bb = ps[6 if hh % 2 == 0 else 7]
                    ctx.wait("pe", f"{qp}.rc{hh}")
                    if hh >= 2:
                        ctx.wait("pe", f"{qp}.bv{hh-2}")
                    elif pq is not None:
                        ctx.wait("pe", f"{pq}.bv{6 + hh}")
                    i = pe.matmul(bb[0:64, 0:128], H["onesc1B"][0:1, 0:64],
                                  A["rrow"][0:1, hh % 2, 0:128],
                                  start=True, stop=True)
                    ctx.inc("pe", i, "pe", tag=f"{qp}.bc{hh}")
                def emit_bv(hh):
                    bb = ps[6 if hh % 2 == 0 else 7]
                    ctx.wait("ve", f"{qp}.bc{hh}")
                    i = ve.tensor_copy(A["bcS"][0:64, hh % 2, 0:128],
                                       bb[0:64, 0:128])
                    ctx.inc("ve", i, "ve", tag=f"{qp}.bv{hh}")
                def emit_ot(hh):
                    lo64 = 64 * (hh % 2)
                    ch = hh // 2
                    ctx.wait("ve", f"{qp}.bv{hh}")
                    if hh == 0 and ppq is not None:
                        ctx.wait("ve", f"{ppq}.omm1")
                    i = ve.tensor_tensor(A["o_in"][lo64:lo64 + 64, qj % 2, ch, 0:128],
                                         A["oraw"][0:64, hh % 4, 0:128],
                                         A["bcS"][0:64, hh % 2, 0:128], ALU.mult)
                    ctx.inc("ve", i, "ve", tag=f"{qp}.ot{hh}")
                for hh in range(11):
                    if hh == 2 and pending[0] is not None:
                        emit_oproj(*pending[0])
                        pending[0] = None
                    if hh < 8:
                        emit_smm(hh)
                        emit_exp(hh)
                        emit_mt(hh)
                    if 2 <= hh < 10:
                        emit_av(hh - 2)
                        emit_orv(hh - 2)
                        emit_rc(hh - 2)
                    if hh >= 3:
                        emit_bc(hh - 3)
                        emit_bv(hh - 3)
                        emit_ot(hh - 3)
                pending[0] = (qp, qj, qlo, qt)
                prev2_qp = prev_qp
                prev_qp = qp
                prev_qp = qp
            pass
        if pending[0] is not None:
            emit_oproj(*pending[0])
            pending[0] = None
        ctx.marks[f"{hp}.lastq"] = ctx.marks[f"{hp}.t4.j15.og1"]
        ctx.marks[f"{hp}.lastprod"] = ctx.marks[f"{hp}.t4.v1.ev"]
    H["_ln_cons"] = (("ve", "att1.lastq"), ("sc", "att1.t4.j15.exp7"),
                     ("pe", "att1.t4.j15.omm1"))


def phase_l1ffn(ctx, H, which):
    sy, gp, pe, sc, ve = (ctx.eng(e) for e in ENGS)
    ps = H["ps"]
    F = H["f1"]
    i = gp.dma_start(out=F["w1q"][:],
                     in_=H["in"]["w1T"].rearrange("(c p) n -> p c n", p=128))
    ctx.inc("gp", i, "dw", 16)
    i = gp.dma_start(out=F["w2q"][:],
                     in_=H["in"]["w2T"].rearrange("(c p) n -> p c n", p=128))
    ctx.inc("gp", i, "dw", 16)
    i = gp.dma_start(out=F["rf8"][0:1, 0:4096], in_=H["in"]["b1r"])
    ctx.inc("gp", i, "dw", 16)
    i = gp.dma_start(out=F["rf8"][0:1, 4096:5120], in_=H["in"]["b2r1"])
    ctx.inc("gp", i, "dw", 16, tag=f"{which}.w")
    for e in ("pe", "ve", "sc"):
        ctx.wait(e, f"{which}.w")
        ctx.wait(e, "init.consts")

    ptp = None
    for ti in range(4):
        o = HALO + 512 * ti
        tp = f"{which}.t{ti}"
        ln_tile(ctx, H, f"{tp}.ln", o, 512, out_bf=F["xq"])
        H["_ln_cons"] = (("ve", f"{tp}.hmm31"),)
        if ptp is not None:
            ctx.wait("sy", f"{ptp}.y7")
        i = sy.dma_start(out=F["dm"][:],
                         in_=H["in"]["dmask"][:, 512 * ti:512 * (ti + 1)]
                         .rearrange("(c p) n -> p c n", p=128))
        ctx.inc("sy", i, "dx", 16, tag=f"{tp}.dm")
        ctx.wait("pe", f"{tp}.ln.done")
        for hc in range(32):
            bank = ps[4 + (hc % 2)]
            if hc >= 2:
                ctx.wait("pe", f"{tp}.h16_{hc-2}")
            elif ptp is not None:
                ctx.wait("pe", f"{ptp}.h16_{30 + hc}")
            for c in range(NCHUNK):
                pe.matmul(bank[:, 0:512], F["w1q"][:, c, 128 * hc:128 * (hc + 1)],
                          F["xq"][:, c, 0:512], start=(c == 0), stop=False)
            i = pe.matmul(bank[:, 0:512],
                          F["rf8"][0:1, 128 * hc:128 * (hc + 1)],
                          H["onesrF8"][0:1, 0:512], start=False, stop=True)
            ctx.inc("pe", i, "pe", tag=f"{tp}.hmm{hc}")
            ctx.wait("sc", f"{tp}.hmm{hc}")
            if hc >= 1:
                ctx.wait("sc", f"{tp}.hq{hc-1}")
            i = sc.activation(F["h16"][:, 0:512], bank[:, 0:512], AF.Relu)
            ctx.inc("sc", i, "sc", tag=f"{tp}.h16_{hc}")
            ctx.wait("ve", f"{tp}.h16_{hc}")
            if hc == 0 and ptp is not None:
                ctx.wait("ve", f"{ptp}.ymm7")
            i = ve.tensor_copy(F["hq"][:, hc, 0:512], F["h16"][:, 0:512])
            ctx.inc("ve", i, "ve", tag=f"{tp}.hq{hc}")
        for oc in range(NCHUNK):
            bank = ps[2 + (oc % 2)]
            ctx.wait("pe", f"{tp}.hq31")
            if oc >= 2:
                ctx.wait("pe", f"{tp}.y16_{oc-2}")
            elif ptp is not None:
                ctx.wait("pe", f"{ptp}.y16_{6 + oc}")
            for hc in range(32):
                pe.matmul(bank[:, 0:512], F["w2q"][:, hc, 128 * oc:128 * (oc + 1)],
                          F["hq"][:, hc, 0:512], start=(hc == 0), stop=False)
            i = pe.matmul(bank[:, 0:512],
                          F["rf8"][0:1, 4096 + 128 * oc:4096 + 128 * (oc + 1)],
                          H["onesrF8"][0:1, 0:512], start=False, stop=True)
            ctx.inc("pe", i, "pe", tag=f"{tp}.ymm{oc}")
            ctx.wait("sc", f"{tp}.ymm{oc}")
            if oc >= 1:
                ctx.wait("sc", f"{tp}.y{oc-1}")
            elif ptp is not None:
                ctx.wait("sc", f"{ptp}.y7")
            i = sc.copy(F["y16"][:, 0:512], bank[:, 0:512])
            ctx.inc("sc", i, "sc", tag=f"{tp}.y16_{oc}")
            ctx.wait("ve", f"{tp}.y16_{oc}")
            ctx.wait("ve", f"{tp}.dm")
            i = ve.tensor_tensor(H["stF"][0:, 0:512], F["y16"][:, 0:512],
                                 F["dm"][:, oc, 0:512], ALU.mult)
            ctx.chain("ve", i)
            i = ve.tensor_tensor(H["x"][:, oc, o:o + 512],
                                 H["x"][:, oc, o:o + 512],
                                 H["stF"][0:, 0:512], ALU.add)
            ctx.inc("ve", i, "ve", tag=f"{tp}.y{oc}")
        H["_ln_cons"] = (("ve", f"{tp}.hmm31"),
                         ("pe", f"{tp}.y16_6"), ("pe", f"{tp}.y16_7"),
                         ("pe", f"{tp}.y7"))
        ptp = tp
